# version 09
# baseline (speedup 1.0000x reference)
import numpy as np
import ml_dtypes
import concourse.bass as bass
import concourse.mybir as mybir
from concourse.bass_utils import run_bass_kernel_spmd
from concourse.aot_env import allow_ici_all_to_all
from concourse.ap import AP as APc

F32 = mybir.dt.float32
BF16 = mybir.dt.bfloat16
I32 = mybir.dt.int32
AF = mybir.ActivationFunctionType
ALU = mybir.AluOpType
AX = mybir.AxisListType
BFD = ml_dtypes.bfloat16

NCORES = 8
TC = 1024
SEQ = 8192
D = 4096
NT = TC // 128
RMS_EPS = 1.1920929e-07
LN_EPS = 1e-5
QK_SCALE = float(1.0 / np.sqrt(128.0))
LN2P = float(np.log(10000.0) / 64.0)
INV2PI = float(1.0 / (2 * np.pi))
CW1 = 6.28125
CW2 = float(2 * np.pi - 6.28125)
HALFPI = float(np.pi / 2)

_DEBUG = False
_BUILD_PHASES = 99


class Sem:
    def __init__(self, nc, name):
        self.h = nc.alloc_semaphore(name)
        self.n = 0

    def inc(self, instr, k=1):
        instr.then_inc(self.h, k)
        self.n += k
        return instr


def bcast_j(ap, reps):
    return APc(ap.tensor, ap.offset, [list(x) for x in ap.ap] + [[0, reps]])


def build_program():
    nc = bass.Bass("TRN2", target_bir_lowering=False)

    x_in = nc.dram_tensor("x", [TC, D], F32, kind="ExternalInput").ap()
    pos0 = nc.dram_tensor("pos0", [1, 1], F32, kind="ExternalInput").ap()
    wqt = nc.dram_tensor("wqt", [48 * 128, 4096], F32, kind="ExternalInput").ap()
    wot = nc.dram_tensor("wot", [1024, 16384], F32, kind="ExternalInput").ap()
    wbr = nc.dram_tensor("wbr", [1024, 1024], F32, kind="ExternalInput").ap()
    wcat_t = nc.dram_tensor("wcat_t", [4096, 24], F32, kind="ExternalInput").ap()
    hbase = nc.dram_tensor("hbase", [1, 24], F32, kind="ExternalInput").ap()
    gate = nc.dram_tensor("gate", [1, 1], F32, kind="ExternalInput").ap()
    out = nc.dram_tensor("out", [TC, D], F32, kind="ExternalOutput").ap()

    a2a_all_in = nc.dram_tensor("a2a_all_in", [6144, 1024], BF16, kind="Internal").ap()
    a2a_all_out = nc.dram_tensor("a2a_all_out", [6144, 1024], BF16, kind="Internal").ap()
    a2a_ao_in = nc.dram_tensor("a2a_ao_in", [4096, 1024], BF16, kind="Internal").ap()
    a2a_ao_out = nc.dram_tensor("a2a_ao_out", [4096, 1024], BF16, kind="Internal").ap()
    g_spill = nc.dram_tensor("g_spill", [TC, D], F32, kind="Internal").ap()

    def sb(name, shape, dt=F32):
        return nc.alloc_sbuf_tensor(name, shape, dt)

    ident = sb("ident", [128, 128], BF16)
    ones_col = sb("ones_col", [128, 1], BF16)
    ones_row = sb("ones_row", [1, 128], F32)
    masks = sb("masks", [128, 4 * 512], BF16)
    wcatT = sb("wcatT", [128, 32 * 24], BF16)
    hbase_bc = sb("hbase_bc", [128, 24], F32)
    g_bc = sb("g_bc", [128, 1], F32)
    g1m_bc = sb("g1m_bc", [128, 1], F32)
    i_rep = sb("i_rep", [128, 128], F32)
    stat_r = sb("stat_r", [128, NT], F32)
    stat_mur = sb("stat_mur", [128, NT], F32)
    stat_rmsr = sb("stat_rmsr", [128, NT], F32)
    logits = sb("logits", [128, NT * 24], F32)
    hres = sb("hres", [128, NT * 16], F32)
    hpre = sb("hpre", [128, NT * 4], F32)
    hpost = sb("hpost", [128, NT * 4], F32)
    c_lneps = sb("c_lneps", [128, 1], F32)
    c_rmseps = sb("c_rmseps", [128, 1], F32)
    import contextlib as _ctl
    _ph01 = _ctl.ExitStack()
    cosT = _ph01.enter_context(nc.sbuf_tensor("cosT", [128, 1024], F32))
    sinT = _ph01.enter_context(nc.sbuf_tensor("sinT", [128, 1024], F32))
    cosTq = _ph01.enter_context(nc.sbuf_tensor("cosTq", [128, 1024], F32))
    sinTq = _ph01.enter_context(nc.sbuf_tensor("sinTq", [128, 1024], F32))
    xn_fm = _ph01.enter_context(nc.sbuf_tensor("xn_fm", [128, 32 * 1024], BF16))

    sd = Sem(nc, "sd")
    s_pe = Sem(nc, "s_pe")
    s_act = Sem(nc, "s_act")
    s_dve = Sem(nc, "s_dve")
    s_wq = Sem(nc, "s_wq")
    s_x = Sem(nc, "s_x")
    s_cc = Sem(nc, "s_cc")
    s_ao = Sem(nc, "s_ao")
    s_q = Sem(nc, "s_q")
    s_o = Sem(nc, "s_o")
    s_vv = Sem(nc, "s_vv")

    def vs(instr):
        s_vv.inc(instr)
        nc.vector.wait_ge(s_vv.h, s_vv.n)
        return instr

    s_sa = Sem(nc, "s_sa")

    def sa(instr):
        s_sa.inc(instr)
        nc.scalar.wait_ge(s_sa.h, s_sa.n)
        return instr

    def wait_all_engines(sem, val):
        for e in (nc.sync, nc.gpsimd, nc.vector, nc.scalar, nc.tensor):
            e.wait_ge(sem.h, val)

    def tap(name, ap):
        th = nc.dram_tensor("tap_" + name, list(ap.shape), ap.dtype, kind="ExternalOutput").ap()
        sd.inc(nc.sync.dma_start(out=th, in_=ap), 16)

    with nc.sbuf_tensor("p0_i32", [128, 1024], I32) as t_i32, \
         nc.sbuf_tensor("p0_f", [128, 1024], F32) as t_f, \
         nc.sbuf_tensor("p0_ang", [64, 1024], F32) as t_ang, \
         nc.sbuf_tensor("p0_a2", [64, 1024], F32) as t_a2, \
         nc.sbuf_tensor("p0_k", [64, 1024], F32) as t_k, \
         nc.sbuf_tensor("p0_ki", [64, 1024], I32) as t_ki, \
         nc.sbuf_tensor("p0_p0", [1, 1], F32) as t_p0, \
         nc.sbuf_tensor("p0_invf", [1, 64], F32) as t_invf, \
         nc.sbuf_tensor("p0_hb", [1, 24], F32) as t_hb, \
         nc.sbuf_tensor("p0_g", [1, 1], F32) as t_g, \
         nc.sbuf_tensor("p0_irow", [1, 128], F32) as t_irow, \
         nc.psum_tensor("p0_ps", [128, 1024], F32) as p0_ps:

        nc.gpsimd.memset(ident[:], 0.0)
        nc.gpsimd.affine_select(out=ident[:], in_=ident[:], compare_op=ALU.not_equal,
                                fill=1.0, base=0, pattern=[[-1, 128]], channel_multiplier=1)
        nc.gpsimd.memset(ones_col[:], 1.0)
        nc.gpsimd.memset(c_lneps[:], LN_EPS)
        nc.gpsimd.memset(c_rmseps[:], RMS_EPS)
        nc.gpsimd.memset(ones_row[:], 1.0)
        nc.gpsimd.memset(masks[:], 1.0)
        for m in range(4):
            nc.gpsimd.affine_select(out=masks[:, m * 512:(m + 1) * 512],
                                    in_=masks[:, m * 512:(m + 1) * 512],
                                    compare_op=ALU.is_ge, fill=0.0,
                                    base=-128 * m, pattern=[[1, 512]], channel_multiplier=-1)
        nc.gpsimd.iota(t_i32[:], pattern=[[1, 1024]], base=0, channel_multiplier=0)
        nc.gpsimd.tensor_copy(t_f[:], t_i32[:])
        nc.gpsimd.memset(t_irow[:], 0.0)
        for g in range(NT):
            for i in range(4):
                nc.gpsimd.memset(t_irow[0:1, 16 * g + 5 * i: 16 * g + 5 * i + 1], 1.0)
        sd.inc(nc.gpsimd.dma_start(out=t_hb[:], in_=hbase), 16)
        sd.inc(nc.gpsimd.dma_start(out=t_g[:], in_=gate), 16)
        sd.inc(nc.gpsimd.dma_start(out=t_p0[:], in_=pos0), 16)
        sd.inc(nc.gpsimd.dma_start(out=wcatT[:].rearrange("p (c o) -> p c o", c=32),
                                   in_=wcat_t.rearrange("(c p) o -> p c o", p=128)), 16)
        nc.all_engine_barrier()
        wait_all_engines(sd, sd.n)

        nc.gpsimd.iota(t_ki[:, 0:1], pattern=[[0, 1]], base=0, channel_multiplier=1)
        nc.all_engine_barrier()
        nc.vector.tensor_copy(t_k[:, 0:1], t_ki[0:64, 0:1])
        nc.all_engine_barrier()
        nc.scalar.activation(t_k[:, 1:2], t_k[:, 0:1], AF.Exp, scale=-LN2P)
        nc.all_engine_barrier()
        nc.tensor.matmul(p0_ps[0:64, 0:1], ones_row[0:1, 0:64], t_p0[:], start=True, stop=True)
        nc.all_engine_barrier()
        vs(nc.vector.tensor_copy(t_k[:, 2:3], p0_ps[0:64, 0:1]))
        nc.vector.tensor_scalar(t_ang[:], t_f[0:64, :], t_k[:, 2:3], None, ALU.add)
        nc.vector.tensor_scalar(t_ang[:], t_ang[:], t_k[:, 1:2], None, ALU.mult)
        nc.all_engine_barrier()

        for dst, shift in ((sinT, 0.0), (cosT, HALFPI)):
            nc.vector.tensor_scalar(t_a2[:], t_ang[:], shift, None, ALU.add)
            nc.vector.tensor_scalar(t_k[:], t_a2[:], INV2PI, None, ALU.mult)
            nc.vector.tensor_copy(t_ki[0:64, :], t_k[:])
            nc.vector.tensor_copy(t_k[:], t_ki[0:64, :])
            nc.vector.scalar_tensor_tensor(t_a2[:], t_k[:], -CW1, t_a2[:], ALU.mult, ALU.add)
            nc.vector.scalar_tensor_tensor(t_a2[:], t_k[:], -CW2, t_a2[:], ALU.mult, ALU.add)
            nc.all_engine_barrier()
            nc.scalar.activation(dst[0:64, :], t_a2[:], AF.Sin)
            nc.all_engine_barrier()
            nc.vector.tensor_copy(dst[64:128, :], dst[0:64, :])
        nc.vector.tensor_scalar(cosTq[:], cosT[:], QK_SCALE, None, ALU.mult)
        nc.vector.tensor_scalar(sinTq[:], sinT[:], QK_SCALE, None, ALU.mult)

        nc.scalar.activation(t_g[:], t_g[:], AF.Exp, scale=-1.0)
        nc.all_engine_barrier()
        vs(nc.vector.tensor_scalar(t_g[:], t_g[:], 1.0, None, ALU.add))
        nc.vector.reciprocal(t_g[:], t_g[:])
        nc.all_engine_barrier()
        nc.tensor.matmul(p0_ps[:, 0:1], ones_row[:], t_g[:], start=True, stop=True)
        nc.tensor.matmul(p0_ps[:, 4:28], ones_row[:], t_hb[:], start=True, stop=True)
        nc.tensor.matmul(p0_ps[:, 128:256], ones_row[:], t_irow[:], start=True, stop=True)
        nc.all_engine_barrier()
        vs(nc.vector.tensor_copy(g_bc[:], p0_ps[:, 0:1]))
        nc.vector.tensor_copy(hbase_bc[:], p0_ps[:, 4:28])
        nc.vector.tensor_copy(i_rep[:], p0_ps[:, 128:256])
        nc.vector.tensor_scalar(g1m_bc[:], g_bc[:], -1.0, 1.0, ALU.mult, ALU.add)
        nc.all_engine_barrier()
        if _DEBUG:
            for nm, ap in (("cosT", cosT[:]), ("sinT", sinT[:]), ("g_bc", g_bc[:]),
                           ("i_rep", i_rep[:]), ("hbase_bc", hbase_bc[:]), ("masks", masks[:])):
                tap(nm, ap)
            wait_all_engines(sd, sd.n)

    if _BUILD_PHASES >= 1:
        with nc.sbuf_tensor("xt", [128, 2 * 4096], F32) as xt, \
             nc.sbuf_tensor("xnt", [128, 4096], F32) as xnt, \
             nc.sbuf_tensor("xnb", [128, 4096], BF16) as xnb, \
             nc.sbuf_tensor("sq_scr", [128, 4096], F32) as sq_scr, \
             nc.sbuf_tensor("ln_s", [128, 8], F32) as ln_s, \
             nc.sbuf_tensor("wq_sb", [128, 2 * 4096], BF16) as wq_sb, \
             nc.sbuf_tensor("rope_t", [64, 4 * 1024], F32) as rope_t, \
             nc.sbuf_tensor("kvq_out", [128, 2 * 1024], BF16) as kvq_out, \
             nc.psum_tensor("p_tx0", [128, 128], BF16) as p_tx0, \
             nc.psum_tensor("p_tx1", [128, 128], BF16) as p_tx1, \
             nc.psum_tensor("p_mm", [128, 2 * 1024], F32) as p_mm, \
             nc.psum_tensor("p_lg0", [128, 24], F32) as p_lg0, \
             nc.psum_tensor("p_lg1", [128, 24], F32) as p_lg1:

            p_tx = (p_tx0, p_tx1)
            p_lg = (p_lg0, p_lg1)

            xt_free = [0, 0]
            tx_evac = [0, 0]
            for tt in range(NT):
                par = tt % 2
                nc.sync.wait_ge(s_dve.h, xt_free[par])
                xtarget = s_x.n + 16
                s_x.inc(nc.sync.dma_start(out=xt[:, par * 4096:(par + 1) * 4096],
                                          in_=x_in[tt * 128:(tt + 1) * 128, :]), 16)
                xtile = xt[:, par * 4096:(par + 1) * 4096]
                nc.vector.wait_ge(s_x.h, xtarget)
                nc.scalar.wait_ge(s_x.h, xtarget)
                vs(nc.vector.reduce_sum(ln_s[:, 0:1], xtile, axis=AX.X))
                vs(nc.vector.tensor_scalar(ln_s[:, 0:1], ln_s[:, 0:1], 1.0 / D, None, ALU.mult))
                sa(nc.scalar.activation(sq_scr[:], xtile, AF.Square, accum_out=ln_s[:, 1:2]))
                s_act.inc(nc.scalar.mul(ln_s[:, 2:3], ln_s[:, 1:2], 1.0 / D))
                nc.vector.wait_ge(s_act.h, s_act.n)
                vs(nc.vector.tensor_tensor(ln_s[:, 3:4], ln_s[:, 0:1], ln_s[:, 0:1], ALU.mult))
                vs(nc.vector.tensor_tensor(ln_s[:, 3:4], ln_s[:, 2:3], ln_s[:, 3:4], ALU.subtract))
                nc.scalar.wait_ge(s_vv.h, s_vv.n)
                sa(nc.scalar.activation(ln_s[:, 4:5], ln_s[:, 3:4], AF.Ln, bias=c_lneps[:]))
                s_act.inc(nc.scalar.activation(stat_r[:, tt:tt + 1], ln_s[:, 4:5], AF.Exp, scale=-0.5))
                nc.vector.wait_ge(s_act.h, s_act.n)
                vs(nc.vector.tensor_tensor(stat_mur[:, tt:tt + 1], ln_s[:, 0:1], stat_r[:, tt:tt + 1], ALU.mult))
                s_dve.inc(nc.vector.tensor_scalar(xnt[:], xtile, stat_r[:, tt:tt + 1],
                                                  stat_mur[:, tt:tt + 1], ALU.mult, ALU.subtract))
                xt_free[par] = s_dve.n
                nc.scalar.wait_ge(s_dve.h, s_dve.n)
                sa(nc.scalar.activation(sq_scr[:], xnt[:], AF.Square, accum_out=ln_s[:, 5:6]))
                sa(nc.scalar.activation(ln_s[:, 6:7], ln_s[:, 5:6], AF.Ln, scale=1.0 / D, bias=c_rmseps[:]))
                s_act.inc(nc.scalar.activation(stat_rmsr[:, tt:tt + 1], ln_s[:, 6:7], AF.Exp, scale=-0.5))
                nc.vector.wait_ge(s_act.h, s_act.n)
                s_dve.inc(nc.vector.tensor_copy(xnb[:], xnt[:]))
                nc.tensor.wait_ge(s_dve.h, s_dve.n)
                for kc in range(32):
                    tp = kc % 2
                    nc.tensor.wait_ge(s_dve.h, tx_evac[tp])
                    s_pe.inc(nc.tensor.transpose(p_tx[tp][:], 
                                                 xnb[:, kc * 128:(kc + 1) * 128], ident[:]))
                    nc.vector.wait_ge(s_pe.h, s_pe.n)
                    s_dve.inc(nc.vector.tensor_copy(
                        xn_fm[:, kc * 1024 + tt * 128: kc * 1024 + (tt + 1) * 128],
                        p_tx[tp][:]))
                    tx_evac[tp] = s_dve.n
            nc.all_engine_barrier()
            if _DEBUG:
                for nm, ap in (("stat_r", stat_r[:]), ("stat_mur", stat_mur[:]),
                               ("stat_rmsr", stat_rmsr[:]), ("xn_fm", xn_fm[:])):
                    tap(nm, ap)
                wait_all_engines(sd, sd.n)

            mm_evac = [0, 0]
            out_free = [0, 0]
            wq_used = [0, 0]
            m_order = list(range(16, 48)) + list(range(16))
            for m_idx, m in enumerate(m_order):
                par = m_idx % 2
                nc.gpsimd.wait_ge(s_pe.h, wq_used[par])
                wtarget = s_wq.n + 16
                s_wq.inc(nc.gpsimd.dma_start(out=wq_sb[:, par * 4096:(par + 1) * 4096],
                                             in_=wqt[m * 128:(m + 1) * 128, :]), 16)
                nc.tensor.wait_ge(s_wq.h, wtarget)
                nc.tensor.wait_ge(s_dve.h, mm_evac[par])
                for nh in range(2):
                    for kc in range(32):
                        last = (nh == 1 and kc == 31)
                        s_pe.inc(nc.tensor.matmul(
                            p_mm[:, par * 1024 + nh * 512: par * 1024 + (nh + 1) * 512],
                            wq_sb[:, par * 4096 + kc * 128: par * 4096 + (kc + 1) * 128],
                            xn_fm[:, kc * 1024 + nh * 512: kc * 1024 + (nh + 1) * 512],
                            start=(kc == 0), stop=(kc == 31)), 1 if last else 0)
                wq_used[par] = s_pe.n
                pm = p_mm[:, par * 1024:(par + 1) * 1024]
                ob = kvq_out[:, par * 1024:(par + 1) * 1024]
                nc.vector.wait_ge(s_pe.h, s_pe.n)
                nc.vector.wait_ge(sd.h, out_free[par])
                if m < 8 or m >= 16:
                    isq = m >= 16
                    ct, st = (cosTq, sinTq) if isq else (cosT, sinT)
                    r = rope_t
                    nc.vector.tensor_tensor(r[:, 0:1024], pm[0:64, :], ct[0:64, :], ALU.mult)
                    nc.vector.tensor_tensor(r[:, 1024:2048], pm[64:128, :], st[0:64, :], ALU.mult)
                    nc.vector.tensor_tensor(r[:, 2048:3072], pm[0:64, :], st[0:64, :], ALU.mult)
                    vs(nc.vector.tensor_tensor(r[:, 3072:4096], pm[64:128, :], ct[0:64, :], ALU.mult))
                    nc.vector.tensor_tensor(ob[0:64, :], r[:, 0:1024], r[:, 1024:2048], ALU.subtract)
                    s_dve.inc(nc.vector.tensor_tensor(ob[64:128, :], r[:, 2048:3072], r[:, 3072:4096], ALU.add))
                    if isq:
                        h = m - 16
                        dst = a2a_all_in[768 * (h // 4) + 256 + 128 * (h % 4):
                                         768 * (h // 4) + 256 + 128 * (h % 4) + 128, :]
                    else:
                        dst = a2a_all_in[768 * m: 768 * m + 128, :]
                else:
                    g = m - 8
                    s_dve.inc(nc.vector.tensor_copy(ob, pm))
                    dst = a2a_all_in[768 * g + 128: 768 * g + 256, :]
                mm_evac[par] = s_dve.n
                nc.sync.wait_ge(s_dve.h, s_dve.n)
                sd.inc(nc.sync.dma_start(out=dst, in_=ob), 16)
                if _DEBUG and m in (0, 8):
                    nc.all_engine_barrier()
                    tap("wq_slab_%d" % m, wq_sb[:, par * 4096:(par + 1) * 4096])
                    tap("ob_%d" % m, ob)
                    wait_all_engines(sd, sd.n)
                    nc.all_engine_barrier()
                out_free[par] = sd.n
            nc.gpsimd.wait_ge(sd.h, sd.n)
            with allow_ici_all_to_all():
                s_cc.inc(nc.gpsimd.collective_compute(
                    "AllToAll", ALU.bypass, replica_groups=[list(range(NCORES))],
                    ins=[a2a_all_in], outs=[a2a_all_out]))

            lg_evac = [0, 0]
            for tt in range(NT):
                par = tt % 2
                nc.tensor.wait_ge(s_vv.h, lg_evac[par])
                for kc in range(32):
                    s_pe.inc(nc.tensor.matmul(
                        p_lg[par][:],
                        xn_fm[:, kc * 1024 + tt * 128: kc * 1024 + (tt + 1) * 128],
                        wcatT[:, kc * 24:(kc + 1) * 24],
                        start=(kc == 0), stop=(kc == 31)), 1 if kc == 31 else 0)
                nc.vector.wait_ge(s_pe.h, s_pe.n)
                vs(nc.vector.tensor_scalar(logits[:, tt * 24:(tt + 1) * 24],
                                            p_lg[par][:],
                                            stat_rmsr[:, tt:tt + 1], None, ALU.mult))
                vs(nc.vector.tensor_tensor(logits[:, tt * 24:(tt + 1) * 24],
                                            logits[:, tt * 24:(tt + 1) * 24],
                                            hbase_bc[:], ALU.add))
                lg_evac[par] = s_vv.n
                out_free[par] = sd.n
            nc.gpsimd.wait_ge(sd.h, sd.n)
            with allow_ici_all_to_all():
                s_cc.inc(nc.gpsimd.collective_compute(
                    "AllToAll", ALU.bypass, replica_groups=[list(range(NCORES))],
                    ins=[a2a_all_in], outs=[a2a_all_out]))
            if _DEBUG:
                wait_all_engines(s_cc, s_cc.n)
                tap("a2a_q_in", a2a_q_in)
                tap("a2a_kv_in", a2a_kv_in)
                tap("a2a_q_out", a2a_q_out)
                tap("a2a_kv_out", a2a_kv_out)
                wait_all_engines(sd, sd.n)
            nc.all_engine_barrier()
            if _DEBUG:
                tap("logits", logits[:])
                wait_all_engines(sd, sd.n)

    _ph01.close()

    if _BUILD_PHASES >= 2:
        with nc.sbuf_tensor("sk_rs2", [128, 64], F32) as rs2, \
             nc.sbuf_tensor("sk_rs", [128, 32], F32) as rs:
            Z = logits[:].rearrange("p (g o) -> p g o", o=24)
            nc.scalar.activation(hres[:], Z[:, :, 0:16], AF.Exp, scale=10.0)
            nc.all_engine_barrier()
            H = hres[:].rearrange("p (g i j) -> p g i j", i=4, j=4)
            Hc = hres[:].rearrange("p (g i j) -> p g j i", i=4, j=4)
            R2i = rs2[:].rearrange("p (g i h) -> p g i h", i=4, h=2)
            Ri = rs[:].rearrange("p (g i) -> p g i", i=4)
            Rib = bcast_j(Ri, 4)
            R2j = rs2[:].rearrange("p (g j h) -> p g j h", j=4, h=2)
            Rj = rs[:].rearrange("p (g j) -> p g j", j=4)
            Rjb = bcast_j(Rj, 4)
            for it in range(50):
                vs(nc.vector.tensor_tensor(R2i, H[:, :, :, 0::2], H[:, :, :, 1::2], ALU.add))
                vs(nc.vector.tensor_tensor(Ri, R2i[:, :, :, 0], R2i[:, :, :, 1], ALU.add))
                vs(nc.vector.reciprocal(rs[:], rs[:]))
                vs(nc.vector.tensor_tensor(H, H, Rib, ALU.mult))
                vs(nc.vector.tensor_tensor(R2j, Hc[:, :, :, 0::2], Hc[:, :, :, 1::2], ALU.add))
                vs(nc.vector.tensor_tensor(Rj, R2j[:, :, :, 0], R2j[:, :, :, 1], ALU.add))
                vs(nc.vector.reciprocal(rs[:], rs[:]))
                vs(nc.vector.tensor_tensor(Hc, Hc, Rjb, ALU.mult))
            vs(nc.vector.tensor_scalar(hres[:], hres[:], g_bc[:], None, ALU.mult))
            vs(nc.vector.scalar_tensor_tensor(hres[:], i_rep[:], g1m_bc[:], hres[:], ALU.mult, ALU.add))
            for dst, off in ((hpre, 16), (hpost, 20)):
                nc.all_engine_barrier()
                nc.scalar.activation(dst[:], Z[:, :, off:off + 4], AF.Exp)
                nc.all_engine_barrier()
                Dv = dst[:].rearrange("p (g j) -> p g j", j=4)
                S2 = rs2[:, 0:16].rearrange("p (g h) -> p g h", h=2)
                S1 = rs[:, 0:8].rearrange("p (g o) -> p g o", o=1)
                vs(nc.vector.tensor_tensor(S2, Dv[:, :, 0::2], Dv[:, :, 1::2], ALU.add))
                vs(nc.vector.tensor_tensor(S1, S2[:, :, 0:1], S2[:, :, 1:2], ALU.add))
                vs(nc.vector.reciprocal(rs[:, 0:8], rs[:, 0:8]))
                nc.vector.tensor_tensor(Dv, Dv, bcast_j(S1[:, :, 0], 4), ALU.mult)
            nc.all_engine_barrier()
            if _DEBUG:
                for nm, ap in (("hres_f", hres[:]), ("hpre", hpre[:]), ("hpost", hpost[:])):
                    tap(nm, ap)
                wait_all_engines(sd, sd.n)

    _ph34 = _ctl.ExitStack()
    wbr_sb = _ph34.enter_context(nc.sbuf_tensor("wbr_sb", [128, 8 * 1024], BF16))
    wo_sb = _ph34.enter_context(nc.sbuf_tensor("wo_sb", [128, 2 * 32 * 512], BF16))
    wtgt4a = [0]
    wo_pre_tgt = [0]
    if _BUILD_PHASES >= 3:
        with nc.sbuf_tensor("K_sb", [128, 8 * 1024], BF16) as K_sb, \
             nc.sbuf_tensor("Vf_sb", [128, 8 * 1024], BF16) as Vf_sb, \
             nc.sbuf_tensor("Vt_sb", [128, 8 * 1024], BF16) as Vt_sb, \
             nc.sbuf_tensor("q_sb", [128, 2 * 8192], BF16) as q_sb, \
             nc.sbuf_tensor("e_sb", [128, 2 * 1024], BF16) as e_sb, \
             nc.sbuf_tensor("den_r", [1, 512], F32) as den_r, \
             nc.sbuf_tensor("bc_sb", [128, 512], F32) as bc_sb, \
             nc.sbuf_tensor("ao_t", [128, 2 * 512], BF16) as ao_t:

            wait_all_engines(s_cc, 1)
            kvv = a2a_all_out.rearrange("(r s) t -> s r t", s=768)
            sd.inc(nc.sync.dma_start(out=K_sb[:].rearrange("d (r t) -> d r t", r=8),
                                     in_=kvv[0:128, :, :]), 16)
            sd.inc(nc.sync.dma_start(out=Vf_sb[:].rearrange("d (r t) -> d r t", r=8),
                                     in_=kvv[128:256, :, :]), 16)
            wtgt4a[0] = s_wq.n + 8 * 16
            for ch in range(8):
                s_wq.inc(nc.gpsimd.dma_start(out=wbr_sb[:, ch * 1024:(ch + 1) * 1024],
                                             in_=wbr[ch * 128:(ch + 1) * 128, :]), 16)
            wo_pre_tgt[0] = s_wq.n + 16
            s_wq.inc(nc.gpsimd.dma_start(out=wo_sb[:, 0:16384], in_=wot[0:128, :]), 16)
            qtgt = [0, 0]
            qtgt[0] = s_q.n + 16
            s_q.inc(nc.sync.dma_start(out=q_sb[:, 0:8192].rearrange("d (r t) -> d r t", r=8),
                                      in_=kvv[256:384, :, :]), 16)
            wait_all_engines(sd, sd.n)
            with nc.psum_tensor("p_vt0", [128, 128], BF16) as p_vt0, \
                 nc.psum_tensor("p_vt1", [128, 128], BF16) as p_vt1:
                p_vt = (p_vt0, p_vt1)
                vt_evac = [0, 0]
                for ch in range(64):
                    tp = ch % 2
                    nc.tensor.wait_ge(s_dve.h, vt_evac[tp])
                    s_pe.inc(nc.tensor.transpose(p_vt[tp][:], Vf_sb[:, ch * 128:(ch + 1) * 128], ident[:]))
                    nc.vector.wait_ge(s_pe.h, s_pe.n)
                    s_dve.inc(nc.vector.tensor_copy(Vt_sb[:, ch * 128:(ch + 1) * 128], p_vt[tp][:]))
                    vt_evac[tp] = s_dve.n
                nc.all_engine_barrier()

            with nc.psum_tensor("ps_s0", [128, 1024], F32) as ps_s0, \
                 nc.psum_tensor("ps_s1", [128, 1024], F32) as ps_s1, \
                 nc.psum_tensor("ps_pv0", [128, 512], F32) as ps_pv0, \
                 nc.psum_tensor("ps_pv1", [128, 512], F32) as ps_pv1, \
                 nc.psum_tensor("ps_den", [1, 512], F32) as ps_den, \
                 nc.psum_tensor("ps_bc", [128, 512], F32) as ps_bc:
                ps_s = (ps_s0, ps_s1)
                ps_pv = (ps_pv0, ps_pv1)
                exp_done = [0, 0]
                e_used = [0, 0]
                mask_done = [0, 0]
                pv_free = [0, 0]
                den_free = 0
                bc_free = 0
                ao_free = [0, 0]
                q_scores_done = [0, 0]
                unit_idx = 0
                for hh in range(4):
                    qpar = hh % 2
                    if hh < 3:
                        nc.sync.wait_ge(s_pe.h, q_scores_done[1 - qpar])
                        qtgt[1 - qpar] = s_q.n + 16
                        s_q.inc(nc.sync.dma_start(
                            out=q_sb[:, (1 - qpar) * 8192:(2 - qpar) * 8192].rearrange("d (r t) -> d r t", r=8),
                            in_=kvv[256 + 128 * (hh + 1):256 + 128 * (hh + 2), :, :]), 16)
                    nc.tensor.wait_ge(s_q.h, qtgt[qpar])
                    for qb in range(16):
                        kcs = 4 * (qb + 1)
                        pvb = unit_idx % 2
                        npairs = kcs // 2
                        scores_cnt = [0, 0]
                        mm_done = 0

                        def emit_scores(up):
                            sp = up % 2
                            nc.tensor.wait_ge(s_act.h, exp_done[sp])
                            u = 2 * up
                            nc.tensor.matmul(ps_s[sp][:, 0:512],
                                             K_sb[:, u * 128:(u + 1) * 128],
                                             q_sb[:, qpar * 8192 + qb * 512: qpar * 8192 + (qb + 1) * 512],
                                             start=True, stop=True)
                            s_pe.inc(nc.tensor.matmul(ps_s[sp][:, 512:1024],
                                             K_sb[:, (u + 1) * 128:(u + 2) * 128],
                                             q_sb[:, qpar * 8192 + qb * 512: qpar * 8192 + (qb + 1) * 512],
                                             start=True, stop=True))
                            scores_cnt[sp] = s_pe.n
                            nc.scalar.wait_ge(s_pe.h, scores_cnt[sp])
                            nc.scalar.wait_ge(s_pe.h, e_used[sp])
                            s_act.inc(nc.scalar.activation(e_sb[:, sp * 1024:(sp + 1) * 1024],
                                                           ps_s[sp][:], AF.Exp))
                            exp_done[sp] = s_act.n
                            diag0 = u - 4 * qb
                            diag1 = u + 1 - 4 * qb
                            m0 = 0 <= diag0 <= 3
                            m1 = 0 <= diag1 <= 3
                            if m0 or m1:
                                nc.vector.wait_ge(s_act.h, exp_done[sp])
                            if m0:
                                i0 = nc.vector.tensor_tensor(
                                    e_sb[:, sp * 1024: sp * 1024 + 512],
                                    e_sb[:, sp * 1024: sp * 1024 + 512],
                                    masks[:, diag0 * 512:(diag0 + 1) * 512], ALU.mult)
                                if not m1:
                                    s_dve.inc(i0)
                                    mask_done[sp] = s_dve.n
                            if m1:
                                s_dve.inc(nc.vector.tensor_tensor(
                                    e_sb[:, sp * 1024 + 512: sp * 1024 + 1024],
                                    e_sb[:, sp * 1024 + 512: sp * 1024 + 1024],
                                    masks[:, diag1 * 512:(diag1 + 1) * 512], ALU.mult))
                                mask_done[sp] = s_dve.n
                            return m0 or m1

                        def emit_denpv(up, masked):
                            nonlocal mm_done
                            sp = up % 2
                            u = 2 * up
                            first = (up == 0)
                            last = (up == npairs - 1)
                            if masked:
                                nc.tensor.wait_ge(s_dve.h, mask_done[sp])
                            else:
                                nc.tensor.wait_ge(s_act.h, exp_done[sp])
                            if first:
                                nc.tensor.wait_ge(s_dve.h, den_free)
                                nc.tensor.wait_ge(s_dve.h, pv_free[pvb])
                            nc.tensor.matmul(ps_den[:], ones_col[:],
                                             e_sb[:, sp * 1024: sp * 1024 + 512],
                                             start=first, stop=False)
                            nc.tensor.matmul(ps_den[:], ones_col[:],
                                             e_sb[:, sp * 1024 + 512: sp * 1024 + 1024],
                                             start=False, stop=last)
                            nc.tensor.matmul(ps_pv[pvb][:], Vt_sb[:, u * 128:(u + 1) * 128],
                                             e_sb[:, sp * 1024: sp * 1024 + 512],
                                             start=first, stop=False)
                            s_pe.inc(nc.tensor.matmul(ps_pv[pvb][:], Vt_sb[:, (u + 1) * 128:(u + 2) * 128],
                                             e_sb[:, sp * 1024 + 512: sp * 1024 + 1024],
                                             start=False, stop=last))
                            e_used[sp] = s_pe.n
                            if last:
                                mm_done = s_pe.n

                        mflags = [False] * npairs
                        mflags[0] = emit_scores(0)
                        for up in range(1, npairs):
                            mflags[up] = emit_scores(up)
                            emit_denpv(up - 1, mflags[up - 1])
                        emit_denpv(npairs - 1, mflags[npairs - 1])
                        nc.vector.wait_ge(s_pe.h, mm_done)
                        s_dve.inc(nc.vector.reciprocal(den_r[:], ps_den[:]))
                        den_free = s_dve.n
                        nc.vector.wait_ge(s_dve.h, den_free)
                        nc.tensor.wait_ge(s_dve.h, den_free)
                        nc.tensor.wait_ge(s_dve.h, bc_free)
                        s_pe.inc(nc.tensor.matmul(ps_bc[:], ones_row[:], den_r[:], start=True, stop=True))
                        nc.vector.wait_ge(s_pe.h, s_pe.n)
                        vs(nc.vector.tensor_copy(bc_sb[:], ps_bc[:]))
                        nc.vector.wait_ge(s_ao.h, ao_free[pvb])
                        s_dve.inc(nc.vector.tensor_tensor(ao_t[:, pvb * 512:(pvb + 1) * 512],
                                                          ps_pv[pvb][:], bc_sb[:], ALU.mult))
                        pv_free[pvb] = s_dve.n
                        bc_free = s_dve.n
                        nc.sync.wait_ge(s_dve.h, s_dve.n)
                        s_ao.inc(nc.sync.dma_start(
                            out=a2a_ao_in[512 * (qb // 2) + 128 * hh: 512 * (qb // 2) + 128 * (hh + 1),
                                          (qb % 2) * 512:(qb % 2 + 1) * 512],
                            in_=ao_t[:, pvb * 512:(pvb + 1) * 512]), 16)
                        ao_free[pvb] = s_ao.n
                        if _DEBUG and hh == 0 and qb <= 1:
                            nc.all_engine_barrier()
                            wait_all_engines(s_ao, s_ao.n)
                            tap("e_sb_%d" % qb, e_sb[:])
                            tap("den_%d" % qb, den_r[:])
                            tap("bc_%d" % qb, bc_sb[:])
                            tap("ao_%d" % qb, ao_t[:])
                            if qb == 0:
                                tap("K_sb", K_sb[:])
                                tap("Vt_sb", Vt_sb[:])
                                tap("q0", q_sb[:, 0:8192])
                            wait_all_engines(sd, sd.n)
                            nc.all_engine_barrier()
                        unit_idx += 1
                    q_scores_done[qpar] = s_pe.n
                nc.gpsimd.wait_ge(s_ao.h, s_ao.n)
                with allow_ici_all_to_all():
                    s_cc.inc(nc.gpsimd.collective_compute(
                        "AllToAll", ALU.bypass, replica_groups=[list(range(NCORES))],
                        ins=[a2a_ao_in], outs=[a2a_ao_out]))
        nc.all_engine_barrier()

    if _BUILD_PHASES >= 4:
        with nc.sbuf_tensor("x4", [128, 2 * 4096], F32) as x4, \
             nc.sbuf_tensor("xn4", [128, 4096], F32) as xn4, \
             nc.sbuf_tensor("xp4", [128, 1024], F32) as xp4, \
             nc.sbuf_tensor("xpb4", [128, 1024], BF16) as xpb4, \
             nc.sbuf_tensor("xpf4", [128, 1024], BF16) as xpf4, \
             nc.sbuf_tensor("y4", [128, 1024], F32) as y4, \
             nc.sbuf_tensor("go4", [128, 2 * 4096], F32) as go4, \
             nc.psum_tensor("p_y", [128, 1024], F32) as p_y, \
             nc.psum_tensor("p_xt0", [128, 128], BF16) as p_xt0, \
             nc.psum_tensor("p_xt1", [128, 128], BF16) as p_xt1:
            p_xt = (p_xt0, p_xt1)
            wtgt = wtgt4a[0]
            x4_free = [0, 0]
            go_free = [0, 0]
            xt_evac = [0, 0]
            for tt in range(NT):
                par = tt % 2
                nc.sync.wait_ge(s_dve.h, x4_free[par])
                xtgt = s_x.n + 16
                s_x.inc(nc.sync.dma_start(out=x4[:, par * 4096:(par + 1) * 4096],
                                          in_=x_in[tt * 128:(tt + 1) * 128, :]), 16)
                xtile = x4[:, par * 4096:(par + 1) * 4096]
                nc.vector.wait_ge(s_x.h, xtgt)
                s_dve.inc(nc.vector.tensor_scalar(xn4[:], xtile, stat_r[:, tt:tt + 1],
                                                  stat_mur[:, tt:tt + 1], ALU.mult, ALU.subtract))
                x4_free[par] = s_dve.n
                nc.vector.tensor_scalar(xp4[:], xn4[:, 0:1024], hpre[:, 4 * tt: 4 * tt + 1], None, ALU.mult)
                for n in range(1, 4):
                    instr = nc.vector.scalar_tensor_tensor(xp4[:], xn4[:, n * 1024:(n + 1) * 1024],
                                                           hpre[:, 4 * tt + n: 4 * tt + n + 1], xp4[:],
                                                           ALU.mult, ALU.add)
                    if n == 3:
                        vs(instr)
                s_dve.inc(nc.vector.tensor_copy(xpb4[:], xp4[:]))
                nc.tensor.wait_ge(s_dve.h, s_dve.n)
                for ch in range(8):
                    tp = ch % 2
                    nc.tensor.wait_ge(s_dve.h, xt_evac[tp])
                    s_pe.inc(nc.tensor.transpose(p_xt[tp][:], xpb4[:, ch * 128:(ch + 1) * 128], ident[:]))
                    nc.vector.wait_ge(s_pe.h, s_pe.n)
                    s_dve.inc(nc.vector.tensor_copy(xpf4[:, ch * 128:(ch + 1) * 128], p_xt[tp][:]))
                    xt_evac[tp] = s_dve.n
                nc.tensor.wait_ge(s_wq.h, wtgt)
                nc.tensor.wait_ge(s_dve.h, s_dve.n)
                for nh in range(2):
                    for ch in range(8):
                        s_pe.inc(nc.tensor.matmul(p_y[:, nh * 512:(nh + 1) * 512],
                                                  xpf4[:, ch * 128:(ch + 1) * 128],
                                                  wbr_sb[:, ch * 1024 + nh * 512: ch * 1024 + (nh + 1) * 512],
                                                  start=(ch == 0), stop=(ch == 7)),
                                 1 if (nh == 1 and ch == 7) else 0)
                nc.vector.wait_ge(s_pe.h, s_pe.n)
                s_dve.inc(nc.vector.tensor_copy(y4[:], p_y[:]))
                nc.vector.wait_ge(s_ao.h, go_free[par])
                go = go4[:, par * 4096:(par + 1) * 4096]
                for i in range(4):
                    nc.vector.tensor_scalar(go[:, i * 1024:(i + 1) * 1024], xn4[:, 0:1024],
                                            hres[:, 16 * tt + 4 * i: 16 * tt + 4 * i + 1], None, ALU.mult)
                    for j in range(1, 4):
                        nc.vector.scalar_tensor_tensor(go[:, i * 1024:(i + 1) * 1024],
                                                       xn4[:, j * 1024:(j + 1) * 1024],
                                                       hres[:, 16 * tt + 4 * i + j: 16 * tt + 4 * i + j + 1],
                                                       go[:, i * 1024:(i + 1) * 1024], ALU.mult, ALU.add)
                    s_dve.inc(nc.vector.scalar_tensor_tensor(go[:, i * 1024:(i + 1) * 1024],
                                                   y4[:], hpost[:, 4 * tt + i: 4 * tt + i + 1],
                                                   go[:, i * 1024:(i + 1) * 1024], ALU.mult, ALU.add))
                nc.scalar.wait_ge(s_dve.h, s_dve.n)
                s_act.inc(nc.scalar.activation(go, go, AF.Silu))
                nc.sync.wait_ge(s_act.h, s_act.n)
                s_ao.inc(nc.sync.dma_start(out=g_spill[tt * 128:(tt + 1) * 128, :], in_=go), 16)
                go_free[par] = s_ao.n
        g_spill_target = s_ao.n
        nc.all_engine_barrier()

    if _BUILD_PHASES >= 5:
        with nc.sbuf_tensor("aosb", [128, 32 * 1024], BF16) as aosb, \
             nc.sbuf_tensor("xs4", [128, 2 * 512], F32) as xs4, \
             nc.sbuf_tensor("gs4", [128, 2 * 512], F32) as gs4, \
             nc.sbuf_tensor("ot4", [128, 2 * 512], F32) as ot4, \
             nc.psum_tensor("p_o0", [128, 512], F32) as p_o0, \
             nc.psum_tensor("p_o1", [128, 512], F32) as p_o1:
            p_o = (p_o0, p_o1)
            wait_all_engines(s_cc, 2)
            nc.sync.wait_ge(s_ao.h, g_spill_target)
            aotgt = sd.n + 16
            sd.inc(nc.sync.dma_start(out=aosb[:].rearrange("p (c t) -> p c t", c=32),
                                     in_=a2a_ao_out.rearrange("(c p) t -> p c t", p=128)), 16)
            nc.tensor.wait_ge(sd.h, aotgt)
            wo_tgt = [0, 0]
            o_evac = [0, 0]
            st_free = [0, 0]
            ot_free = [0, 0]
            wo_used = [0, 0]
            wo_tgt[0] = wo_pre_tgt[0]
            for og in range(8):
                wpar = og % 2
                if og > 0:
                    nc.gpsimd.wait_ge(s_pe.h, wo_used[wpar])
                    wo_tgt[wpar] = s_wq.n + 16
                    s_wq.inc(nc.gpsimd.dma_start(
                        out=wo_sb[:, wpar * 16384:(wpar + 1) * 16384],
                        in_=wot[og * 128:(og + 1) * 128, :]), 16)
                nc.tensor.wait_ge(s_wq.h, wo_tgt[wpar])
                for qt in range(NT):
                    opar = qt % 2
                    nc.tensor.wait_ge(s_dve.h, o_evac[opar])
                    for dc in range(32):
                        s_pe.inc(nc.tensor.matmul(
                            p_o[opar][:],
                            aosb[:, dc * 1024 + qt * 128: dc * 1024 + (qt + 1) * 128],
                            wo_sb[:, wpar * 16384 + dc * 512: wpar * 16384 + (dc + 1) * 512],
                            start=(dc == 0), stop=(dc == 31)), 1 if dc == 31 else 0)
                    mmcnt = s_pe.n
                    nc.sync.wait_ge(s_dve.h, st_free[opar])
                    stgt = s_x.n + 32
                    s_x.inc(nc.sync.dma_start(out=xs4[:, opar * 512:(opar + 1) * 512],
                                              in_=x_in[qt * 128:(qt + 1) * 128, og * 512:(og + 1) * 512]), 16)
                    s_x.inc(nc.sync.dma_start(out=gs4[:, opar * 512:(opar + 1) * 512],
                                              in_=g_spill[qt * 128:(qt + 1) * 128, og * 512:(og + 1) * 512]), 16)
                    nc.vector.wait_ge(s_pe.h, mmcnt)
                    nc.vector.wait_ge(s_x.h, stgt)
                    nc.vector.wait_ge(s_o.h, ot_free[opar])
                    vs(nc.vector.tensor_tensor(ot4[:, opar * 512:(opar + 1) * 512],
                                               p_o[opar][:], gs4[:, opar * 512:(opar + 1) * 512], ALU.mult))
                    s_dve.inc(nc.vector.tensor_tensor(ot4[:, opar * 512:(opar + 1) * 512],
                                            ot4[:, opar * 512:(opar + 1) * 512],
                                            xs4[:, opar * 512:(opar + 1) * 512], ALU.add))
                    o_evac[opar] = s_dve.n
                    st_free[opar] = s_dve.n
                    nc.sync.wait_ge(s_dve.h, s_dve.n)
                    s_o.inc(nc.sync.dma_start(out=out[qt * 128:(qt + 1) * 128, og * 512:(og + 1) * 512],
                                              in_=ot4[:, opar * 512:(opar + 1) * 512]), 16)
                    ot_free[opar] = s_o.n
                wo_used[wpar] = s_pe.n
            wait_all_engines(s_o, s_o.n)
        nc.all_engine_barrier()
    _ph34.close()

    wait_all_engines(sd, sd.n)
    if _BUILD_PHASES < 5:
        if s_cc.n:
            wait_all_engines(s_cc, s_cc.n)
        with nc.sbuf_tensor("zout", [128, D], F32) as zout:
            nc.vector.memset(zout[:], 0.0)
            nc.all_engine_barrier()
            for tt in range(NT):
                sd.inc(nc.sync.dma_start(out=out[tt * 128:(tt + 1) * 128, :], in_=zout[:]), 16)
            wait_all_engines(sd, sd.n)
    return nc, out


def _host_layouts(inputs):
    w_qkv = np.asarray(inputs["L0_attn_w_qkv"], np.float32)
    w_o = np.asarray(inputs["L0_attn_w_o"], np.float32)
    W_res = np.asarray(inputs["L0_ffn_W_res"], np.float32)
    W_pre = np.asarray(inputs["L0_ffn_W_pre"], np.float32)
    W_post = np.asarray(inputs["L0_ffn_W_post"], np.float32)

    deint = np.concatenate([np.arange(0, 128, 2), np.arange(1, 128, 2)])
    cols = []
    for g in range(8):
        cols.append(4096 + 128 * g + deint)
    for g in range(8):
        cols.append(5120 + 128 * g + np.arange(128))
    for h in range(32):
        cols.append(128 * h + deint)
    col_order = np.concatenate(cols)
    wq = w_qkv[:, col_order]
    wqt = np.ascontiguousarray(
        wq.reshape(32, 128, 48, 128).transpose(2, 1, 0, 3).reshape(48 * 128, 32 * 128))
    wot = np.ascontiguousarray(
        w_o.reshape(32, 128, 8, 512).transpose(2, 1, 0, 3).reshape(1024, 16384))
    wcat_t = np.ascontiguousarray(np.concatenate([W_res, W_pre, W_post], 0).T)
    hbase = np.ascontiguousarray(np.concatenate([
        np.asarray(inputs["L0_ffn_H_res_base"], np.float32).reshape(-1),
        np.asarray(inputs["L0_ffn_H_pre_base"], np.float32).reshape(-1),
        np.asarray(inputs["L0_ffn_H_post_base"], np.float32).reshape(-1)])[None, :])
    gate = np.asarray(inputs["L0_ffn_gate"], np.float32).reshape(1, 1)
    wbr = np.ascontiguousarray(np.asarray(inputs["L0_ffn_W_branch"], np.float32))
    return wqt, wot, wbr, wcat_t, hbase, gate


def make_in_maps(inputs):
    x = np.asarray(inputs["L0_attn_x"], np.float32).reshape(SEQ, D)
    wqt, wot, wbr, wcat_t, hbase, gate = _host_layouts(inputs)
    in_maps = []
    for c in range(NCORES):
        in_maps.append({
            "x": np.ascontiguousarray(x[c * TC:(c + 1) * TC]),
            "pos0": np.array([[c * TC]], np.float32),
            "wqt": wqt, "wot": wot, "wbr": wbr, "wcat_t": wcat_t,
            "hbase": hbase, "gate": gate,
        })
    return in_maps


def kernel(**inputs):
    nc, _ = build_program()
    in_maps = make_in_maps(inputs)
    res = run_bass_kernel_spmd(nc, in_maps, core_ids=list(range(NCORES)))
    out = np.empty((SEQ, D), np.float32)
    for c in range(NCORES):
        out[c * TC:(c + 1) * TC] = res.results[c]["out"]
    return out.reshape(1, SEQ, D)


# version 10 (final)
# speedup vs baseline: 1.0101x; 1.0101x over previous
import numpy as np
import ml_dtypes
import concourse.bass as bass
import concourse.mybir as mybir
from concourse.bass_utils import run_bass_kernel_spmd
from concourse.aot_env import allow_ici_all_to_all
from concourse.ap import AP as APc

F32 = mybir.dt.float32
BF16 = mybir.dt.bfloat16
I32 = mybir.dt.int32
AF = mybir.ActivationFunctionType
ALU = mybir.AluOpType
AX = mybir.AxisListType
BFD = ml_dtypes.bfloat16

NCORES = 8
TC = 1024
SEQ = 8192
D = 4096
NT = TC // 128
RMS_EPS = 1.1920929e-07
LN_EPS = 1e-5
QK_SCALE = float(1.0 / np.sqrt(128.0))
LN2P = float(np.log(10000.0) / 64.0)
INV2PI = float(1.0 / (2 * np.pi))
CW1 = 6.28125
CW2 = float(2 * np.pi - 6.28125)
HALFPI = float(np.pi / 2)

_DEBUG = False
_BUILD_PHASES = 99


class Sem:
    def __init__(self, nc, name):
        self.h = nc.alloc_semaphore(name)
        self.n = 0

    def inc(self, instr, k=1):
        instr.then_inc(self.h, k)
        self.n += k
        return instr


def bcast_j(ap, reps):
    return APc(ap.tensor, ap.offset, [list(x) for x in ap.ap] + [[0, reps]])


def build_program():
    nc = bass.Bass("TRN2", target_bir_lowering=False)

    x_in = nc.dram_tensor("x", [TC, D], F32, kind="ExternalInput").ap()
    pos0 = nc.dram_tensor("pos0", [1, 1], F32, kind="ExternalInput").ap()
    wqt = nc.dram_tensor("wqt", [48 * 128, 4096], F32, kind="ExternalInput").ap()
    wot = nc.dram_tensor("wot", [1024, 16384], F32, kind="ExternalInput").ap()
    wbr = nc.dram_tensor("wbr", [1024, 1024], F32, kind="ExternalInput").ap()
    wcat_t = nc.dram_tensor("wcat_t", [4096, 24], F32, kind="ExternalInput").ap()
    hbase = nc.dram_tensor("hbase", [1, 24], F32, kind="ExternalInput").ap()
    gate = nc.dram_tensor("gate", [1, 1], F32, kind="ExternalInput").ap()
    out = nc.dram_tensor("out", [TC, D], F32, kind="ExternalOutput").ap()

    a2a_all_in = nc.dram_tensor("a2a_all_in", [6144, 1024], BF16, kind="Internal").ap()
    a2a_all_out = nc.dram_tensor("a2a_all_out", [6144, 1024], BF16, kind="Internal").ap()
    a2a_ao_in = nc.dram_tensor("a2a_ao_in", [4096, 1024], BF16, kind="Internal").ap()
    a2a_ao_out = nc.dram_tensor("a2a_ao_out", [4096, 1024], BF16, kind="Internal").ap()
    g_spill = nc.dram_tensor("g_spill", [TC, D], F32, kind="Internal").ap()

    def sb(name, shape, dt=F32):
        return nc.alloc_sbuf_tensor(name, shape, dt)

    ident = sb("ident", [128, 128], BF16)
    ones_col = sb("ones_col", [128, 1], BF16)
    ones_row = sb("ones_row", [1, 128], F32)
    masks = sb("masks", [128, 4 * 512], BF16)
    wcatT = sb("wcatT", [128, 32 * 24], BF16)
    hbase_bc = sb("hbase_bc", [128, 24], F32)
    g_bc = sb("g_bc", [128, 1], F32)
    g1m_bc = sb("g1m_bc", [128, 1], F32)
    i_rep = sb("i_rep", [128, 128], F32)
    stat_r = sb("stat_r", [128, NT], F32)
    stat_mur = sb("stat_mur", [128, NT], F32)
    stat_rmsr = sb("stat_rmsr", [128, NT], F32)
    logits = sb("logits", [128, NT * 24], F32)
    hres = sb("hres", [128, NT * 16], F32)
    hpre = sb("hpre", [128, NT * 4], F32)
    hpost = sb("hpost", [128, NT * 4], F32)
    c_lneps = sb("c_lneps", [128, 1], F32)
    c_rmseps = sb("c_rmseps", [128, 1], F32)
    import contextlib as _ctl
    _ph01 = _ctl.ExitStack()
    cosT = _ph01.enter_context(nc.sbuf_tensor("cosT", [128, 1024], F32))
    sinT = _ph01.enter_context(nc.sbuf_tensor("sinT", [128, 1024], F32))
    cosTq = _ph01.enter_context(nc.sbuf_tensor("cosTq", [128, 1024], F32))
    sinTq = _ph01.enter_context(nc.sbuf_tensor("sinTq", [128, 1024], F32))
    xn_fm = _ph01.enter_context(nc.sbuf_tensor("xn_fm", [128, 32 * 1024], BF16))

    sd = Sem(nc, "sd")
    s_pe = Sem(nc, "s_pe")
    s_act = Sem(nc, "s_act")
    s_dve = Sem(nc, "s_dve")
    s_wq = Sem(nc, "s_wq")
    s_x = Sem(nc, "s_x")
    s_cc = Sem(nc, "s_cc")
    s_ao = Sem(nc, "s_ao")
    s_q = Sem(nc, "s_q")
    s_o = Sem(nc, "s_o")
    s_vv = Sem(nc, "s_vv")

    def vs(instr):
        s_vv.inc(instr)
        nc.vector.wait_ge(s_vv.h, s_vv.n)
        return instr

    s_sa = Sem(nc, "s_sa")

    def sa(instr):
        s_sa.inc(instr)
        nc.scalar.wait_ge(s_sa.h, s_sa.n)
        return instr

    def wait_all_engines(sem, val):
        for e in (nc.sync, nc.gpsimd, nc.vector, nc.scalar, nc.tensor):
            e.wait_ge(sem.h, val)

    def tap(name, ap):
        th = nc.dram_tensor("tap_" + name, list(ap.shape), ap.dtype, kind="ExternalOutput").ap()
        sd.inc(nc.sync.dma_start(out=th, in_=ap), 16)

    with nc.sbuf_tensor("p0_i32", [128, 1024], I32) as t_i32, \
         nc.sbuf_tensor("p0_f", [128, 1024], F32) as t_f, \
         nc.sbuf_tensor("p0_ang", [64, 1024], F32) as t_ang, \
         nc.sbuf_tensor("p0_a2", [64, 1024], F32) as t_a2, \
         nc.sbuf_tensor("p0_k", [64, 1024], F32) as t_k, \
         nc.sbuf_tensor("p0_ki", [64, 1024], I32) as t_ki, \
         nc.sbuf_tensor("p0_p0", [1, 1], F32) as t_p0, \
         nc.sbuf_tensor("p0_invf", [1, 64], F32) as t_invf, \
         nc.sbuf_tensor("p0_hb", [1, 24], F32) as t_hb, \
         nc.sbuf_tensor("p0_g", [1, 1], F32) as t_g, \
         nc.sbuf_tensor("p0_irow", [1, 128], F32) as t_irow, \
         nc.psum_tensor("p0_ps", [128, 1024], F32) as p0_ps:

        nc.gpsimd.memset(ident[:], 0.0)
        nc.gpsimd.affine_select(out=ident[:], in_=ident[:], compare_op=ALU.not_equal,
                                fill=1.0, base=0, pattern=[[-1, 128]], channel_multiplier=1)
        nc.gpsimd.memset(ones_col[:], 1.0)
        nc.gpsimd.memset(c_lneps[:], LN_EPS)
        nc.gpsimd.memset(c_rmseps[:], RMS_EPS)
        nc.gpsimd.memset(ones_row[:], 1.0)
        nc.gpsimd.memset(masks[:], 1.0)
        for m in range(4):
            nc.gpsimd.affine_select(out=masks[:, m * 512:(m + 1) * 512],
                                    in_=masks[:, m * 512:(m + 1) * 512],
                                    compare_op=ALU.is_ge, fill=0.0,
                                    base=-128 * m, pattern=[[1, 512]], channel_multiplier=-1)
        nc.gpsimd.iota(t_i32[:], pattern=[[1, 1024]], base=0, channel_multiplier=0)
        nc.gpsimd.tensor_copy(t_f[:], t_i32[:])
        nc.gpsimd.memset(t_irow[:], 0.0)
        for g in range(NT):
            for i in range(4):
                nc.gpsimd.memset(t_irow[0:1, 16 * g + 5 * i: 16 * g + 5 * i + 1], 1.0)
        sd.inc(nc.gpsimd.dma_start(out=t_hb[:], in_=hbase), 16)
        sd.inc(nc.gpsimd.dma_start(out=t_g[:], in_=gate), 16)
        sd.inc(nc.gpsimd.dma_start(out=t_p0[:], in_=pos0), 16)
        sd.inc(nc.gpsimd.dma_start(out=wcatT[:].rearrange("p (c o) -> p c o", c=32),
                                   in_=wcat_t.rearrange("(c p) o -> p c o", p=128)), 16)
        nc.all_engine_barrier()
        wait_all_engines(sd, sd.n)

        nc.gpsimd.iota(t_ki[:, 0:1], pattern=[[0, 1]], base=0, channel_multiplier=1)
        nc.all_engine_barrier()
        nc.vector.tensor_copy(t_k[:, 0:1], t_ki[0:64, 0:1])
        nc.all_engine_barrier()
        nc.scalar.activation(t_k[:, 1:2], t_k[:, 0:1], AF.Exp, scale=-LN2P)
        nc.all_engine_barrier()
        nc.tensor.matmul(p0_ps[0:64, 0:1], ones_row[0:1, 0:64], t_p0[:], start=True, stop=True)
        nc.all_engine_barrier()
        vs(nc.vector.tensor_copy(t_k[:, 2:3], p0_ps[0:64, 0:1]))
        nc.vector.tensor_scalar(t_ang[:], t_f[0:64, :], t_k[:, 2:3], None, ALU.add)
        nc.vector.tensor_scalar(t_ang[:], t_ang[:], t_k[:, 1:2], None, ALU.mult)
        nc.all_engine_barrier()

        for dst, shift in ((sinT, 0.0), (cosT, HALFPI)):
            nc.vector.tensor_scalar(t_a2[:], t_ang[:], shift, None, ALU.add)
            nc.vector.tensor_scalar(t_k[:], t_a2[:], INV2PI, None, ALU.mult)
            nc.vector.tensor_copy(t_ki[0:64, :], t_k[:])
            nc.vector.tensor_copy(t_k[:], t_ki[0:64, :])
            nc.vector.scalar_tensor_tensor(t_a2[:], t_k[:], -CW1, t_a2[:], ALU.mult, ALU.add)
            nc.vector.scalar_tensor_tensor(t_a2[:], t_k[:], -CW2, t_a2[:], ALU.mult, ALU.add)
            nc.all_engine_barrier()
            nc.scalar.activation(dst[0:64, :], t_a2[:], AF.Sin)
            nc.all_engine_barrier()
            nc.vector.tensor_copy(dst[64:128, :], dst[0:64, :])
        nc.vector.tensor_scalar(cosTq[:], cosT[:], QK_SCALE, None, ALU.mult)
        nc.vector.tensor_scalar(sinTq[:], sinT[:], QK_SCALE, None, ALU.mult)

        nc.scalar.activation(t_g[:], t_g[:], AF.Exp, scale=-1.0)
        nc.all_engine_barrier()
        vs(nc.vector.tensor_scalar(t_g[:], t_g[:], 1.0, None, ALU.add))
        nc.vector.reciprocal(t_g[:], t_g[:])
        nc.all_engine_barrier()
        nc.tensor.matmul(p0_ps[:, 0:1], ones_row[:], t_g[:], start=True, stop=True)
        nc.tensor.matmul(p0_ps[:, 4:28], ones_row[:], t_hb[:], start=True, stop=True)
        nc.tensor.matmul(p0_ps[:, 128:256], ones_row[:], t_irow[:], start=True, stop=True)
        nc.all_engine_barrier()
        vs(nc.vector.tensor_copy(g_bc[:], p0_ps[:, 0:1]))
        nc.vector.tensor_copy(hbase_bc[:], p0_ps[:, 4:28])
        nc.vector.tensor_copy(i_rep[:], p0_ps[:, 128:256])
        nc.vector.tensor_scalar(g1m_bc[:], g_bc[:], -1.0, 1.0, ALU.mult, ALU.add)
        nc.all_engine_barrier()
        if _DEBUG:
            for nm, ap in (("cosT", cosT[:]), ("sinT", sinT[:]), ("g_bc", g_bc[:]),
                           ("i_rep", i_rep[:]), ("hbase_bc", hbase_bc[:]), ("masks", masks[:])):
                tap(nm, ap)
            wait_all_engines(sd, sd.n)

    if _BUILD_PHASES >= 1:
        with nc.sbuf_tensor("xt", [128, 2 * 4096], F32) as xt, \
             nc.sbuf_tensor("xnt", [128, 4096], F32) as xnt, \
             nc.sbuf_tensor("xnb", [128, 4096], BF16) as xnb, \
             nc.sbuf_tensor("sq_scr", [128, 4096], F32) as sq_scr, \
             nc.sbuf_tensor("ln_s", [128, 8], F32) as ln_s, \
             nc.sbuf_tensor("wq_sb", [128, 2 * 4096], BF16) as wq_sb, \
             nc.sbuf_tensor("rope_t", [64, 4 * 1024], F32) as rope_t, \
             nc.sbuf_tensor("kvq_out", [128, 2 * 1024], BF16) as kvq_out, \
             nc.psum_tensor("p_tx0", [128, 128], BF16) as p_tx0, \
             nc.psum_tensor("p_tx1", [128, 128], BF16) as p_tx1, \
             nc.psum_tensor("p_mm", [128, 2 * 1024], F32) as p_mm, \
             nc.psum_tensor("p_lg0", [128, 24], F32) as p_lg0, \
             nc.psum_tensor("p_lg1", [128, 24], F32) as p_lg1:

            p_tx = (p_tx0, p_tx1)
            p_lg = (p_lg0, p_lg1)

            xt_free = [0, 0]
            tx_evac = [0, 0]
            for tt in range(NT):
                par = tt % 2
                nc.sync.wait_ge(s_dve.h, xt_free[par])
                xtarget = s_x.n + 16
                s_x.inc(nc.sync.dma_start(out=xt[:, par * 4096:(par + 1) * 4096],
                                          in_=x_in[tt * 128:(tt + 1) * 128, :]), 16)
                xtile = xt[:, par * 4096:(par + 1) * 4096]
                nc.vector.wait_ge(s_x.h, xtarget)
                nc.scalar.wait_ge(s_x.h, xtarget)
                vs(nc.vector.reduce_sum(ln_s[:, 0:1], xtile, axis=AX.X))
                vs(nc.vector.tensor_scalar(ln_s[:, 0:1], ln_s[:, 0:1], 1.0 / D, None, ALU.mult))
                sa(nc.scalar.activation(sq_scr[:], xtile, AF.Square, accum_out=ln_s[:, 1:2]))
                s_act.inc(nc.scalar.mul(ln_s[:, 2:3], ln_s[:, 1:2], 1.0 / D))
                nc.vector.wait_ge(s_act.h, s_act.n)
                vs(nc.vector.tensor_tensor(ln_s[:, 3:4], ln_s[:, 0:1], ln_s[:, 0:1], ALU.mult))
                vs(nc.vector.tensor_tensor(ln_s[:, 3:4], ln_s[:, 2:3], ln_s[:, 3:4], ALU.subtract))
                nc.scalar.wait_ge(s_vv.h, s_vv.n)
                sa(nc.scalar.activation(ln_s[:, 4:5], ln_s[:, 3:4], AF.Ln, bias=c_lneps[:]))
                s_act.inc(nc.scalar.activation(stat_r[:, tt:tt + 1], ln_s[:, 4:5], AF.Exp, scale=-0.5))
                nc.vector.wait_ge(s_act.h, s_act.n)
                vs(nc.vector.tensor_tensor(stat_mur[:, tt:tt + 1], ln_s[:, 0:1], stat_r[:, tt:tt + 1], ALU.mult))
                s_dve.inc(nc.vector.tensor_scalar(xnt[:], xtile, stat_r[:, tt:tt + 1],
                                                  stat_mur[:, tt:tt + 1], ALU.mult, ALU.subtract))
                xt_free[par] = s_dve.n
                nc.scalar.wait_ge(s_dve.h, s_dve.n)
                sa(nc.scalar.activation(sq_scr[:], xnt[:], AF.Square, accum_out=ln_s[:, 5:6]))
                sa(nc.scalar.activation(ln_s[:, 6:7], ln_s[:, 5:6], AF.Ln, scale=1.0 / D, bias=c_rmseps[:]))
                s_act.inc(nc.scalar.activation(stat_rmsr[:, tt:tt + 1], ln_s[:, 6:7], AF.Exp, scale=-0.5))
                nc.vector.wait_ge(s_act.h, s_act.n)
                s_dve.inc(nc.vector.tensor_copy(xnb[:], xnt[:]))
                nc.tensor.wait_ge(s_dve.h, s_dve.n)
                for kc in range(32):
                    tp = kc % 2
                    nc.tensor.wait_ge(s_dve.h, tx_evac[tp])
                    s_pe.inc(nc.tensor.transpose(p_tx[tp][:], 
                                                 xnb[:, kc * 128:(kc + 1) * 128], ident[:]))
                    nc.vector.wait_ge(s_pe.h, s_pe.n)
                    s_dve.inc(nc.vector.tensor_copy(
                        xn_fm[:, kc * 1024 + tt * 128: kc * 1024 + (tt + 1) * 128],
                        p_tx[tp][:]))
                    tx_evac[tp] = s_dve.n
            nc.all_engine_barrier()
            if _DEBUG:
                for nm, ap in (("stat_r", stat_r[:]), ("stat_mur", stat_mur[:]),
                               ("stat_rmsr", stat_rmsr[:]), ("xn_fm", xn_fm[:])):
                    tap(nm, ap)
                wait_all_engines(sd, sd.n)

            mm_evac = [0, 0]
            out_free = [0, 0]
            wq_used = [0, 0]
            m_order = list(range(16, 48)) + list(range(16))
            for m_idx, m in enumerate(m_order):
                par = m_idx % 2
                nc.gpsimd.wait_ge(s_pe.h, wq_used[par])
                wtarget = s_wq.n + 16
                s_wq.inc(nc.gpsimd.dma_start(out=wq_sb[:, par * 4096:(par + 1) * 4096],
                                             in_=wqt[m * 128:(m + 1) * 128, :]), 16)
                nc.tensor.wait_ge(s_wq.h, wtarget)
                nc.tensor.wait_ge(s_dve.h, mm_evac[par])
                for nh in range(2):
                    for kc in range(32):
                        last = (nh == 1 and kc == 31)
                        s_pe.inc(nc.tensor.matmul(
                            p_mm[:, par * 1024 + nh * 512: par * 1024 + (nh + 1) * 512],
                            wq_sb[:, par * 4096 + kc * 128: par * 4096 + (kc + 1) * 128],
                            xn_fm[:, kc * 1024 + nh * 512: kc * 1024 + (nh + 1) * 512],
                            start=(kc == 0), stop=(kc == 31)), 1 if last else 0)
                wq_used[par] = s_pe.n
                pm = p_mm[:, par * 1024:(par + 1) * 1024]
                ob = kvq_out[:, par * 1024:(par + 1) * 1024]
                nc.vector.wait_ge(s_pe.h, s_pe.n)
                nc.vector.wait_ge(sd.h, out_free[par])
                if m < 8 or m >= 16:
                    isq = m >= 16
                    ct, st = (cosTq, sinTq) if isq else (cosT, sinT)
                    r = rope_t
                    nc.vector.tensor_tensor(r[:, 0:1024], pm[0:64, :], ct[0:64, :], ALU.mult)
                    nc.vector.tensor_tensor(r[:, 1024:2048], pm[64:128, :], st[0:64, :], ALU.mult)
                    nc.vector.tensor_tensor(r[:, 2048:3072], pm[0:64, :], st[0:64, :], ALU.mult)
                    vs(nc.vector.tensor_tensor(r[:, 3072:4096], pm[64:128, :], ct[0:64, :], ALU.mult))
                    nc.vector.tensor_tensor(ob[0:64, :], r[:, 0:1024], r[:, 1024:2048], ALU.subtract)
                    s_dve.inc(nc.vector.tensor_tensor(ob[64:128, :], r[:, 2048:3072], r[:, 3072:4096], ALU.add))
                    if isq:
                        h = m - 16
                        dst = a2a_all_in[768 * (h // 4) + 256 + 128 * (h % 4):
                                         768 * (h // 4) + 256 + 128 * (h % 4) + 128, :]
                    else:
                        dst = a2a_all_in[768 * m: 768 * m + 128, :]
                else:
                    g = m - 8
                    s_dve.inc(nc.vector.tensor_copy(ob, pm))
                    dst = a2a_all_in[768 * g + 128: 768 * g + 256, :]
                mm_evac[par] = s_dve.n
                nc.sync.wait_ge(s_dve.h, s_dve.n)
                sd.inc(nc.sync.dma_start(out=dst, in_=ob), 16)
                if _DEBUG and m in (0, 8):
                    nc.all_engine_barrier()
                    tap("wq_slab_%d" % m, wq_sb[:, par * 4096:(par + 1) * 4096])
                    tap("ob_%d" % m, ob)
                    wait_all_engines(sd, sd.n)
                    nc.all_engine_barrier()
                out_free[par] = sd.n
            nc.gpsimd.wait_ge(sd.h, sd.n)
            with allow_ici_all_to_all():
                s_cc.inc(nc.gpsimd.collective_compute(
                    "AllToAll", ALU.bypass, replica_groups=[list(range(NCORES))],
                    ins=[a2a_all_in], outs=[a2a_all_out]))

            lg_evac = [0, 0]
            for tt in range(NT):
                par = tt % 2
                nc.tensor.wait_ge(s_vv.h, lg_evac[par])
                for kc in range(32):
                    s_pe.inc(nc.tensor.matmul(
                        p_lg[par][:],
                        xn_fm[:, kc * 1024 + tt * 128: kc * 1024 + (tt + 1) * 128],
                        wcatT[:, kc * 24:(kc + 1) * 24],
                        start=(kc == 0), stop=(kc == 31)), 1 if kc == 31 else 0)
                nc.vector.wait_ge(s_pe.h, s_pe.n)
                vs(nc.vector.tensor_scalar(logits[:, tt * 24:(tt + 1) * 24],
                                            p_lg[par][:],
                                            stat_rmsr[:, tt:tt + 1], None, ALU.mult))
                vs(nc.vector.tensor_tensor(logits[:, tt * 24:(tt + 1) * 24],
                                            logits[:, tt * 24:(tt + 1) * 24],
                                            hbase_bc[:], ALU.add))
                lg_evac[par] = s_vv.n
                out_free[par] = sd.n
            nc.gpsimd.wait_ge(sd.h, sd.n)
            with allow_ici_all_to_all():
                s_cc.inc(nc.gpsimd.collective_compute(
                    "AllToAll", ALU.bypass, replica_groups=[list(range(NCORES))],
                    ins=[a2a_all_in], outs=[a2a_all_out]))
            if _DEBUG:
                wait_all_engines(s_cc, s_cc.n)
                tap("a2a_q_in", a2a_q_in)
                tap("a2a_kv_in", a2a_kv_in)
                tap("a2a_q_out", a2a_q_out)
                tap("a2a_kv_out", a2a_kv_out)
                wait_all_engines(sd, sd.n)
            nc.all_engine_barrier()
            if _DEBUG:
                tap("logits", logits[:])
                wait_all_engines(sd, sd.n)

    _ph01.close()

    if _BUILD_PHASES >= 2:
        with nc.sbuf_tensor("sk_rs2", [128, 64], F32) as rs2, \
             nc.sbuf_tensor("sk_rs", [128, 32], F32) as rs:
            Z = logits[:].rearrange("p (g o) -> p g o", o=24)
            nc.scalar.activation(hres[:], Z[:, :, 0:16], AF.Exp, scale=10.0)
            nc.all_engine_barrier()
            H = hres[:].rearrange("p (g i j) -> p g i j", i=4, j=4)
            Hc = hres[:].rearrange("p (g i j) -> p g j i", i=4, j=4)
            R2i = rs2[:].rearrange("p (g i h) -> p g i h", i=4, h=2)
            Ri = rs[:].rearrange("p (g i) -> p g i", i=4)
            Rib = bcast_j(Ri, 4)
            R2j = rs2[:].rearrange("p (g j h) -> p g j h", j=4, h=2)
            Rj = rs[:].rearrange("p (g j) -> p g j", j=4)
            Rjb = bcast_j(Rj, 4)
            for it in range(50):
                vs(nc.vector.tensor_tensor(R2i, H[:, :, :, 0::2], H[:, :, :, 1::2], ALU.add))
                vs(nc.vector.tensor_tensor(Ri, R2i[:, :, :, 0], R2i[:, :, :, 1], ALU.add))
                vs(nc.vector.reciprocal(rs[:], rs[:]))
                vs(nc.vector.tensor_tensor(H, H, Rib, ALU.mult))
                vs(nc.vector.tensor_tensor(R2j, Hc[:, :, :, 0::2], Hc[:, :, :, 1::2], ALU.add))
                vs(nc.vector.tensor_tensor(Rj, R2j[:, :, :, 0], R2j[:, :, :, 1], ALU.add))
                vs(nc.vector.reciprocal(rs[:], rs[:]))
                vs(nc.vector.tensor_tensor(Hc, Hc, Rjb, ALU.mult))
            vs(nc.vector.tensor_scalar(hres[:], hres[:], g_bc[:], None, ALU.mult))
            vs(nc.vector.scalar_tensor_tensor(hres[:], i_rep[:], g1m_bc[:], hres[:], ALU.mult, ALU.add))
            for dst, off in ((hpre, 16), (hpost, 20)):
                nc.all_engine_barrier()
                nc.scalar.activation(dst[:], Z[:, :, off:off + 4], AF.Exp)
                nc.all_engine_barrier()
                Dv = dst[:].rearrange("p (g j) -> p g j", j=4)
                S2 = rs2[:, 0:16].rearrange("p (g h) -> p g h", h=2)
                S1 = rs[:, 0:8].rearrange("p (g o) -> p g o", o=1)
                vs(nc.vector.tensor_tensor(S2, Dv[:, :, 0::2], Dv[:, :, 1::2], ALU.add))
                vs(nc.vector.tensor_tensor(S1, S2[:, :, 0:1], S2[:, :, 1:2], ALU.add))
                vs(nc.vector.reciprocal(rs[:, 0:8], rs[:, 0:8]))
                nc.vector.tensor_tensor(Dv, Dv, bcast_j(S1[:, :, 0], 4), ALU.mult)
            nc.all_engine_barrier()
            if _DEBUG:
                for nm, ap in (("hres_f", hres[:]), ("hpre", hpre[:]), ("hpost", hpost[:])):
                    tap(nm, ap)
                wait_all_engines(sd, sd.n)

    _ph34 = _ctl.ExitStack()
    wbr_sb = _ph34.enter_context(nc.sbuf_tensor("wbr_sb", [128, 8 * 1024], BF16))
    wtgt4a = [0]
    if _BUILD_PHASES >= 3:
        with nc.sbuf_tensor("K_sb", [128, 8 * 1024], BF16) as K_sb, \
             nc.sbuf_tensor("Vf_sb", [128, 8 * 1024], BF16) as Vf_sb, \
             nc.sbuf_tensor("Vt_sb", [128, 8 * 1024], BF16) as Vt_sb, \
             nc.sbuf_tensor("q_sb", [128, 2 * 8192], BF16) as q_sb, \
             nc.sbuf_tensor("e_sb", [128, 2 * 1024], BF16) as e_sb, \
             nc.sbuf_tensor("den_r", [1, 512], F32) as den_r, \
             nc.sbuf_tensor("bc_sb", [128, 512], F32) as bc_sb, \
             nc.sbuf_tensor("ao_t", [128, 2 * 512], BF16) as ao_t:

            wait_all_engines(s_cc, 1)
            kvv = a2a_all_out.rearrange("(r s) t -> s r t", s=768)
            sd.inc(nc.sync.dma_start(out=K_sb[:].rearrange("d (r t) -> d r t", r=8),
                                     in_=kvv[0:128, :, :]), 16)
            sd.inc(nc.sync.dma_start(out=Vf_sb[:].rearrange("d (r t) -> d r t", r=8),
                                     in_=kvv[128:256, :, :]), 16)
            wtgt4a[0] = s_wq.n + 8 * 16
            for ch in range(8):
                s_wq.inc(nc.gpsimd.dma_start(out=wbr_sb[:, ch * 1024:(ch + 1) * 1024],
                                             in_=wbr[ch * 128:(ch + 1) * 128, :]), 16)
            qtgt = [0, 0]
            qtgt[0] = s_q.n + 16
            s_q.inc(nc.sync.dma_start(out=q_sb[:, 0:8192].rearrange("d (r t) -> d r t", r=8),
                                      in_=kvv[256:384, :, :]), 16)
            wait_all_engines(sd, sd.n)
            with nc.psum_tensor("p_vt0", [128, 128], BF16) as p_vt0, \
                 nc.psum_tensor("p_vt1", [128, 128], BF16) as p_vt1:
                p_vt = (p_vt0, p_vt1)
                vt_evac = [0, 0]
                for ch in range(64):
                    tp = ch % 2
                    nc.tensor.wait_ge(s_dve.h, vt_evac[tp])
                    s_pe.inc(nc.tensor.transpose(p_vt[tp][:], Vf_sb[:, ch * 128:(ch + 1) * 128], ident[:]))
                    nc.vector.wait_ge(s_pe.h, s_pe.n)
                    s_dve.inc(nc.vector.tensor_copy(Vt_sb[:, ch * 128:(ch + 1) * 128], p_vt[tp][:]))
                    vt_evac[tp] = s_dve.n
                nc.all_engine_barrier()

            with nc.psum_tensor("ps_s0", [128, 1024], F32) as ps_s0, \
                 nc.psum_tensor("ps_s1", [128, 1024], F32) as ps_s1, \
                 nc.psum_tensor("ps_pv0", [128, 512], F32) as ps_pv0, \
                 nc.psum_tensor("ps_pv1", [128, 512], F32) as ps_pv1, \
                 nc.psum_tensor("ps_den", [1, 512], F32) as ps_den, \
                 nc.psum_tensor("ps_bc", [128, 512], F32) as ps_bc:
                ps_s = (ps_s0, ps_s1)
                ps_pv = (ps_pv0, ps_pv1)
                exp_done = [0, 0]
                e_used = [0, 0]
                mask_done = [0, 0]
                pv_free = [0, 0]
                den_free = 0
                bc_free = 0
                ao_free = [0, 0]
                q_scores_done = [0, 0]
                unit_idx = 0
                for hh in range(4):
                    qpar = hh % 2
                    if hh < 3:
                        nc.sync.wait_ge(s_pe.h, q_scores_done[1 - qpar])
                        qtgt[1 - qpar] = s_q.n + 16
                        s_q.inc(nc.sync.dma_start(
                            out=q_sb[:, (1 - qpar) * 8192:(2 - qpar) * 8192].rearrange("d (r t) -> d r t", r=8),
                            in_=kvv[256 + 128 * (hh + 1):256 + 128 * (hh + 2), :, :]), 16)
                    nc.tensor.wait_ge(s_q.h, qtgt[qpar])
                    for qb in range(16):
                        kcs = 4 * (qb + 1)
                        pvb = unit_idx % 2
                        npairs = kcs // 2
                        scores_cnt = [0, 0]
                        mm_done = 0

                        def emit_scores(up):
                            sp = up % 2
                            nc.tensor.wait_ge(s_act.h, exp_done[sp])
                            u = 2 * up
                            nc.tensor.matmul(ps_s[sp][:, 0:512],
                                             K_sb[:, u * 128:(u + 1) * 128],
                                             q_sb[:, qpar * 8192 + qb * 512: qpar * 8192 + (qb + 1) * 512],
                                             start=True, stop=True)
                            s_pe.inc(nc.tensor.matmul(ps_s[sp][:, 512:1024],
                                             K_sb[:, (u + 1) * 128:(u + 2) * 128],
                                             q_sb[:, qpar * 8192 + qb * 512: qpar * 8192 + (qb + 1) * 512],
                                             start=True, stop=True))
                            scores_cnt[sp] = s_pe.n
                            nc.scalar.wait_ge(s_pe.h, scores_cnt[sp])
                            nc.scalar.wait_ge(s_pe.h, e_used[sp])
                            s_act.inc(nc.scalar.activation(e_sb[:, sp * 1024:(sp + 1) * 1024],
                                                           ps_s[sp][:], AF.Exp))
                            exp_done[sp] = s_act.n
                            diag0 = u - 4 * qb
                            diag1 = u + 1 - 4 * qb
                            m0 = 0 <= diag0 <= 3
                            m1 = 0 <= diag1 <= 3
                            if m0 or m1:
                                nc.vector.wait_ge(s_act.h, exp_done[sp])
                            if m0:
                                i0 = nc.vector.tensor_tensor(
                                    e_sb[:, sp * 1024: sp * 1024 + 512],
                                    e_sb[:, sp * 1024: sp * 1024 + 512],
                                    masks[:, diag0 * 512:(diag0 + 1) * 512], ALU.mult)
                                if not m1:
                                    s_dve.inc(i0)
                                    mask_done[sp] = s_dve.n
                            if m1:
                                s_dve.inc(nc.vector.tensor_tensor(
                                    e_sb[:, sp * 1024 + 512: sp * 1024 + 1024],
                                    e_sb[:, sp * 1024 + 512: sp * 1024 + 1024],
                                    masks[:, diag1 * 512:(diag1 + 1) * 512], ALU.mult))
                                mask_done[sp] = s_dve.n
                            return m0 or m1

                        def emit_denpv(up, masked):
                            nonlocal mm_done
                            sp = up % 2
                            u = 2 * up
                            first = (up == 0)
                            last = (up == npairs - 1)
                            if masked:
                                nc.tensor.wait_ge(s_dve.h, mask_done[sp])
                            else:
                                nc.tensor.wait_ge(s_act.h, exp_done[sp])
                            if first:
                                nc.tensor.wait_ge(s_dve.h, den_free)
                                nc.tensor.wait_ge(s_dve.h, pv_free[pvb])
                            nc.tensor.matmul(ps_den[:], ones_col[:],
                                             e_sb[:, sp * 1024: sp * 1024 + 512],
                                             start=first, stop=False)
                            nc.tensor.matmul(ps_den[:], ones_col[:],
                                             e_sb[:, sp * 1024 + 512: sp * 1024 + 1024],
                                             start=False, stop=last)
                            nc.tensor.matmul(ps_pv[pvb][:], Vt_sb[:, u * 128:(u + 1) * 128],
                                             e_sb[:, sp * 1024: sp * 1024 + 512],
                                             start=first, stop=False)
                            s_pe.inc(nc.tensor.matmul(ps_pv[pvb][:], Vt_sb[:, (u + 1) * 128:(u + 2) * 128],
                                             e_sb[:, sp * 1024 + 512: sp * 1024 + 1024],
                                             start=False, stop=last))
                            e_used[sp] = s_pe.n
                            if last:
                                mm_done = s_pe.n

                        mflags = [False] * npairs
                        mflags[0] = emit_scores(0)
                        for up in range(1, npairs):
                            mflags[up] = emit_scores(up)
                            emit_denpv(up - 1, mflags[up - 1])
                        emit_denpv(npairs - 1, mflags[npairs - 1])
                        nc.vector.wait_ge(s_pe.h, mm_done)
                        s_dve.inc(nc.vector.reciprocal(den_r[:], ps_den[:]))
                        den_free = s_dve.n
                        nc.vector.wait_ge(s_dve.h, den_free)
                        nc.tensor.wait_ge(s_dve.h, den_free)
                        nc.tensor.wait_ge(s_dve.h, bc_free)
                        s_pe.inc(nc.tensor.matmul(ps_bc[:], ones_row[:], den_r[:], start=True, stop=True))
                        nc.vector.wait_ge(s_pe.h, s_pe.n)
                        vs(nc.vector.tensor_copy(bc_sb[:], ps_bc[:]))
                        nc.vector.wait_ge(s_ao.h, ao_free[pvb])
                        s_dve.inc(nc.vector.tensor_tensor(ao_t[:, pvb * 512:(pvb + 1) * 512],
                                                          ps_pv[pvb][:], bc_sb[:], ALU.mult))
                        pv_free[pvb] = s_dve.n
                        bc_free = s_dve.n
                        nc.sync.wait_ge(s_dve.h, s_dve.n)
                        s_ao.inc(nc.sync.dma_start(
                            out=a2a_ao_in[512 * (qb // 2) + 128 * hh: 512 * (qb // 2) + 128 * (hh + 1),
                                          (qb % 2) * 512:(qb % 2 + 1) * 512],
                            in_=ao_t[:, pvb * 512:(pvb + 1) * 512]), 16)
                        ao_free[pvb] = s_ao.n
                        if _DEBUG and hh == 0 and qb <= 1:
                            nc.all_engine_barrier()
                            wait_all_engines(s_ao, s_ao.n)
                            tap("e_sb_%d" % qb, e_sb[:])
                            tap("den_%d" % qb, den_r[:])
                            tap("bc_%d" % qb, bc_sb[:])
                            tap("ao_%d" % qb, ao_t[:])
                            if qb == 0:
                                tap("K_sb", K_sb[:])
                                tap("Vt_sb", Vt_sb[:])
                                tap("q0", q_sb[:, 0:8192])
                            wait_all_engines(sd, sd.n)
                            nc.all_engine_barrier()
                        unit_idx += 1
                    q_scores_done[qpar] = s_pe.n
                nc.gpsimd.wait_ge(s_ao.h, s_ao.n)
                with allow_ici_all_to_all():
                    s_cc.inc(nc.gpsimd.collective_compute(
                        "AllToAll", ALU.bypass, replica_groups=[list(range(NCORES))],
                        ins=[a2a_ao_in], outs=[a2a_ao_out]))
        nc.all_engine_barrier()

    if _BUILD_PHASES >= 4:
        with nc.sbuf_tensor("x4", [128, 2 * 4096], F32) as x4, \
             nc.sbuf_tensor("xn4", [128, 4096], F32) as xn4, \
             nc.sbuf_tensor("xp4", [128, 1024], F32) as xp4, \
             nc.sbuf_tensor("xpb4", [128, 1024], BF16) as xpb4, \
             nc.sbuf_tensor("xpf4", [128, 1024], BF16) as xpf4, \
             nc.sbuf_tensor("y4", [128, 1024], F32) as y4, \
             nc.sbuf_tensor("go4", [128, 2 * 4096], F32) as go4, \
             nc.psum_tensor("p_y", [128, 1024], F32) as p_y, \
             nc.psum_tensor("p_xt0", [128, 128], BF16) as p_xt0, \
             nc.psum_tensor("p_xt1", [128, 128], BF16) as p_xt1:
            p_xt = (p_xt0, p_xt1)
            wtgt = wtgt4a[0]
            x4_free = [0, 0]
            go_free = [0, 0]
            xt_evac = [0, 0]
            for tt in range(NT):
                par = tt % 2
                nc.sync.wait_ge(s_dve.h, x4_free[par])
                xtgt = s_x.n + 16
                s_x.inc(nc.sync.dma_start(out=x4[:, par * 4096:(par + 1) * 4096],
                                          in_=x_in[tt * 128:(tt + 1) * 128, :]), 16)
                xtile = x4[:, par * 4096:(par + 1) * 4096]
                nc.vector.wait_ge(s_x.h, xtgt)
                s_dve.inc(nc.vector.tensor_scalar(xn4[:], xtile, stat_r[:, tt:tt + 1],
                                                  stat_mur[:, tt:tt + 1], ALU.mult, ALU.subtract))
                x4_free[par] = s_dve.n
                nc.vector.tensor_scalar(xp4[:], xn4[:, 0:1024], hpre[:, 4 * tt: 4 * tt + 1], None, ALU.mult)
                for n in range(1, 4):
                    instr = nc.vector.scalar_tensor_tensor(xp4[:], xn4[:, n * 1024:(n + 1) * 1024],
                                                           hpre[:, 4 * tt + n: 4 * tt + n + 1], xp4[:],
                                                           ALU.mult, ALU.add)
                    if n == 3:
                        vs(instr)
                s_dve.inc(nc.vector.tensor_copy(xpb4[:], xp4[:]))
                nc.tensor.wait_ge(s_dve.h, s_dve.n)
                for ch in range(8):
                    tp = ch % 2
                    nc.tensor.wait_ge(s_dve.h, xt_evac[tp])
                    s_pe.inc(nc.tensor.transpose(p_xt[tp][:], xpb4[:, ch * 128:(ch + 1) * 128], ident[:]))
                    nc.vector.wait_ge(s_pe.h, s_pe.n)
                    s_dve.inc(nc.vector.tensor_copy(xpf4[:, ch * 128:(ch + 1) * 128], p_xt[tp][:]))
                    xt_evac[tp] = s_dve.n
                nc.tensor.wait_ge(s_wq.h, wtgt)
                nc.tensor.wait_ge(s_dve.h, s_dve.n)
                for nh in range(2):
                    for ch in range(8):
                        s_pe.inc(nc.tensor.matmul(p_y[:, nh * 512:(nh + 1) * 512],
                                                  xpf4[:, ch * 128:(ch + 1) * 128],
                                                  wbr_sb[:, ch * 1024 + nh * 512: ch * 1024 + (nh + 1) * 512],
                                                  start=(ch == 0), stop=(ch == 7)),
                                 1 if (nh == 1 and ch == 7) else 0)
                nc.vector.wait_ge(s_pe.h, s_pe.n)
                s_dve.inc(nc.vector.tensor_copy(y4[:], p_y[:]))
                nc.vector.wait_ge(s_ao.h, go_free[par])
                go = go4[:, par * 4096:(par + 1) * 4096]
                for i in range(4):
                    nc.vector.tensor_scalar(go[:, i * 1024:(i + 1) * 1024], xn4[:, 0:1024],
                                            hres[:, 16 * tt + 4 * i: 16 * tt + 4 * i + 1], None, ALU.mult)
                    for j in range(1, 4):
                        nc.vector.scalar_tensor_tensor(go[:, i * 1024:(i + 1) * 1024],
                                                       xn4[:, j * 1024:(j + 1) * 1024],
                                                       hres[:, 16 * tt + 4 * i + j: 16 * tt + 4 * i + j + 1],
                                                       go[:, i * 1024:(i + 1) * 1024], ALU.mult, ALU.add)
                    s_dve.inc(nc.vector.scalar_tensor_tensor(go[:, i * 1024:(i + 1) * 1024],
                                                   y4[:], hpost[:, 4 * tt + i: 4 * tt + i + 1],
                                                   go[:, i * 1024:(i + 1) * 1024], ALU.mult, ALU.add))
                nc.scalar.wait_ge(s_dve.h, s_dve.n)
                s_act.inc(nc.scalar.activation(go, go, AF.Silu))
                nc.sync.wait_ge(s_act.h, s_act.n)
                s_ao.inc(nc.sync.dma_start(out=g_spill[tt * 128:(tt + 1) * 128, :], in_=go), 16)
                go_free[par] = s_ao.n
        g_spill_target = s_ao.n
        nc.all_engine_barrier()
    _ph34.close()

    if _BUILD_PHASES >= 5:
        with nc.sbuf_tensor("aosb", [128, 32 * 1024], BF16) as aosb, \
             nc.sbuf_tensor("wo_sb", [128, 3 * 32 * 512], BF16) as wo_sb, \
             nc.sbuf_tensor("xs4", [128, 4 * 512], F32) as xs4, \
             nc.sbuf_tensor("gs4", [128, 4 * 512], F32) as gs4, \
             nc.sbuf_tensor("ot4", [128, 4 * 512], F32) as ot4, \
             nc.psum_tensor("p_o0", [128, 512], F32) as p_o0, \
             nc.psum_tensor("p_o1", [128, 512], F32) as p_o1, \
             nc.psum_tensor("p_o2", [128, 512], F32) as p_o2, \
             nc.psum_tensor("p_o3", [128, 512], F32) as p_o3:
            p_o = (p_o0, p_o1, p_o2, p_o3)
            wait_all_engines(s_cc, 2)
            nc.sync.wait_ge(s_ao.h, g_spill_target)
            aotgt = sd.n + 16
            sd.inc(nc.sync.dma_start(out=aosb[:].rearrange("p (c t) -> p c t", c=32),
                                     in_=a2a_ao_out.rearrange("(c p) t -> p c t", p=128)), 16)
            nc.tensor.wait_ge(sd.h, aotgt)
            wo_tgt = [0, 0, 0]
            o_evac = [0, 0, 0, 0]
            st_free = [0, 0, 0, 0]
            ot_free = [0, 0, 0, 0]
            wo_used = [0, 0, 0]
            for og in range(8):
                wpar = og % 3
                nc.gpsimd.wait_ge(s_pe.h, wo_used[wpar])
                wo_tgt[wpar] = s_wq.n + 16
                s_wq.inc(nc.gpsimd.dma_start(
                    out=wo_sb[:, wpar * 16384:(wpar + 1) * 16384],
                    in_=wot[og * 128:(og + 1) * 128, :]), 16)
                nc.tensor.wait_ge(s_wq.h, wo_tgt[wpar])
                for qt in range(NT):
                    opar = qt % 4
                    nc.tensor.wait_ge(s_dve.h, o_evac[opar])
                    for dc in range(32):
                        s_pe.inc(nc.tensor.matmul(
                            p_o[opar][:],
                            aosb[:, dc * 1024 + qt * 128: dc * 1024 + (qt + 1) * 128],
                            wo_sb[:, wpar * 16384 + dc * 512: wpar * 16384 + (dc + 1) * 512],
                            start=(dc == 0), stop=(dc == 31)), 1 if dc == 31 else 0)
                    mmcnt = s_pe.n
                    nc.sync.wait_ge(s_dve.h, st_free[opar])
                    stgt = s_x.n + 32
                    s_x.inc(nc.sync.dma_start(out=xs4[:, opar * 512:(opar + 1) * 512],
                                              in_=x_in[qt * 128:(qt + 1) * 128, og * 512:(og + 1) * 512]), 16)
                    s_x.inc(nc.sync.dma_start(out=gs4[:, opar * 512:(opar + 1) * 512],
                                              in_=g_spill[qt * 128:(qt + 1) * 128, og * 512:(og + 1) * 512]), 16)
                    nc.vector.wait_ge(s_pe.h, mmcnt)
                    nc.vector.wait_ge(s_x.h, stgt)
                    nc.vector.wait_ge(s_o.h, ot_free[opar])
                    vs(nc.vector.tensor_tensor(ot4[:, opar * 512:(opar + 1) * 512],
                                               p_o[opar][:], gs4[:, opar * 512:(opar + 1) * 512], ALU.mult))
                    s_dve.inc(nc.vector.tensor_tensor(ot4[:, opar * 512:(opar + 1) * 512],
                                            ot4[:, opar * 512:(opar + 1) * 512],
                                            xs4[:, opar * 512:(opar + 1) * 512], ALU.add))
                    o_evac[opar] = s_dve.n
                    st_free[opar] = s_dve.n
                    nc.sync.wait_ge(s_dve.h, s_dve.n)
                    s_o.inc(nc.sync.dma_start(out=out[qt * 128:(qt + 1) * 128, og * 512:(og + 1) * 512],
                                              in_=ot4[:, opar * 512:(opar + 1) * 512]), 16)
                    ot_free[opar] = s_o.n
                wo_used[wpar] = s_pe.n
            wait_all_engines(s_o, s_o.n)
        nc.all_engine_barrier()

    wait_all_engines(sd, sd.n)
    if _BUILD_PHASES < 5:
        if s_cc.n:
            wait_all_engines(s_cc, s_cc.n)
        with nc.sbuf_tensor("zout", [128, D], F32) as zout:
            nc.vector.memset(zout[:], 0.0)
            nc.all_engine_barrier()
            for tt in range(NT):
                sd.inc(nc.sync.dma_start(out=out[tt * 128:(tt + 1) * 128, :], in_=zout[:]), 16)
            wait_all_engines(sd, sd.n)
    return nc, out


def _host_layouts(inputs):
    w_qkv = np.asarray(inputs["L0_attn_w_qkv"], np.float32)
    w_o = np.asarray(inputs["L0_attn_w_o"], np.float32)
    W_res = np.asarray(inputs["L0_ffn_W_res"], np.float32)
    W_pre = np.asarray(inputs["L0_ffn_W_pre"], np.float32)
    W_post = np.asarray(inputs["L0_ffn_W_post"], np.float32)

    deint = np.concatenate([np.arange(0, 128, 2), np.arange(1, 128, 2)])
    cols = []
    for g in range(8):
        cols.append(4096 + 128 * g + deint)
    for g in range(8):
        cols.append(5120 + 128 * g + np.arange(128))
    for h in range(32):
        cols.append(128 * h + deint)
    col_order = np.concatenate(cols)
    wq = w_qkv[:, col_order]
    wqt = np.ascontiguousarray(
        wq.reshape(32, 128, 48, 128).transpose(2, 1, 0, 3).reshape(48 * 128, 32 * 128))
    wot = np.ascontiguousarray(
        w_o.reshape(32, 128, 8, 512).transpose(2, 1, 0, 3).reshape(1024, 16384))
    wcat_t = np.ascontiguousarray(np.concatenate([W_res, W_pre, W_post], 0).T)
    hbase = np.ascontiguousarray(np.concatenate([
        np.asarray(inputs["L0_ffn_H_res_base"], np.float32).reshape(-1),
        np.asarray(inputs["L0_ffn_H_pre_base"], np.float32).reshape(-1),
        np.asarray(inputs["L0_ffn_H_post_base"], np.float32).reshape(-1)])[None, :])
    gate = np.asarray(inputs["L0_ffn_gate"], np.float32).reshape(1, 1)
    wbr = np.ascontiguousarray(np.asarray(inputs["L0_ffn_W_branch"], np.float32))
    return wqt, wot, wbr, wcat_t, hbase, gate


def make_in_maps(inputs):
    x = np.asarray(inputs["L0_attn_x"], np.float32).reshape(SEQ, D)
    wqt, wot, wbr, wcat_t, hbase, gate = _host_layouts(inputs)
    in_maps = []
    for c in range(NCORES):
        in_maps.append({
            "x": np.ascontiguousarray(x[c * TC:(c + 1) * TC]),
            "pos0": np.array([[c * TC]], np.float32),
            "wqt": wqt, "wot": wot, "wbr": wbr, "wcat_t": wcat_t,
            "hbase": hbase, "gate": gate,
        })
    return in_maps


def kernel(**inputs):
    nc, _ = build_program()
    in_maps = make_in_maps(inputs)
    res = run_bass_kernel_spmd(nc, in_maps, core_ids=list(range(NCORES)))
    out = np.empty((SEQ, D), np.float32)
    for c in range(NCORES):
        out[c * TC:(c + 1) * TC] = res.results[c]["out"]
    return out.reshape(1, SEQ, D)


# version 11
# speedup vs baseline: 1.0254x; 1.0152x over previous
import numpy as np
import ml_dtypes
import concourse.bass as bass
import concourse.mybir as mybir
from concourse.bass_utils import run_bass_kernel_spmd
from concourse.aot_env import allow_ici_all_to_all
from concourse.ap import AP as APc

F32 = mybir.dt.float32
BF16 = mybir.dt.bfloat16
I32 = mybir.dt.int32
AF = mybir.ActivationFunctionType
ALU = mybir.AluOpType
AX = mybir.AxisListType
BFD = ml_dtypes.bfloat16

NCORES = 8
TC = 1024
SEQ = 8192
D = 4096
NT = TC // 128
RMS_EPS = 1.1920929e-07
LN_EPS = 1e-5
QK_SCALE = float(1.0 / np.sqrt(128.0))
LN2P = float(np.log(10000.0) / 64.0)
INV2PI = float(1.0 / (2 * np.pi))
CW1 = 6.28125
CW2 = float(2 * np.pi - 6.28125)
HALFPI = float(np.pi / 2)

_DEBUG = False
_BUILD_PHASES = 99


class Sem:
    def __init__(self, nc, name):
        self.h = nc.alloc_semaphore(name)
        self.n = 0

    def inc(self, instr, k=1):
        instr.then_inc(self.h, k)
        self.n += k
        return instr


def bcast_j(ap, reps):
    return APc(ap.tensor, ap.offset, [list(x) for x in ap.ap] + [[0, reps]])


def build_program():
    nc = bass.Bass("TRN2", target_bir_lowering=False)

    x_in = nc.dram_tensor("x", [TC, D], F32, kind="ExternalInput").ap()
    pos0 = nc.dram_tensor("pos0", [1, 1], F32, kind="ExternalInput").ap()
    wqt = nc.dram_tensor("wqt", [48 * 128, 4096], F32, kind="ExternalInput").ap()
    wot = nc.dram_tensor("wot", [1024, 16384], F32, kind="ExternalInput").ap()
    wbr = nc.dram_tensor("wbr", [1024, 1024], F32, kind="ExternalInput").ap()
    wcat_t = nc.dram_tensor("wcat_t", [4096, 24], F32, kind="ExternalInput").ap()
    hbase = nc.dram_tensor("hbase", [1, 24], F32, kind="ExternalInput").ap()
    gate = nc.dram_tensor("gate", [1, 1], F32, kind="ExternalInput").ap()
    out = nc.dram_tensor("out", [TC, D], F32, kind="ExternalOutput").ap()

    a2a_all_in = nc.dram_tensor("a2a_all_in", [6144, 1024], BF16, kind="Internal").ap()
    a2a_all_out = nc.dram_tensor("a2a_all_out", [6144, 1024], BF16, kind="Internal").ap()
    a2a_ao_in = nc.dram_tensor("a2a_ao_in", [4096, 1024], BF16, kind="Internal").ap()
    a2a_ao_out = nc.dram_tensor("a2a_ao_out", [4096, 1024], BF16, kind="Internal").ap()
    g_spill = nc.dram_tensor("g_spill", [TC, D], F32, kind="Internal").ap()

    def sb(name, shape, dt=F32):
        return nc.alloc_sbuf_tensor(name, shape, dt)

    ident = sb("ident", [128, 128], BF16)
    ones_col = sb("ones_col", [128, 1], BF16)
    ones_row = sb("ones_row", [1, 128], F32)
    masks = sb("masks", [128, 4 * 512], BF16)
    wcatT = sb("wcatT", [128, 32 * 24], BF16)
    hbase_bc = sb("hbase_bc", [128, 24], F32)
    g_bc = sb("g_bc", [128, 1], F32)
    g1m_bc = sb("g1m_bc", [128, 1], F32)
    i_rep = sb("i_rep", [128, 128], F32)
    stat_r = sb("stat_r", [128, NT], F32)
    stat_mur = sb("stat_mur", [128, NT], F32)
    stat_rmsr = sb("stat_rmsr", [128, NT], F32)
    logits = sb("logits", [128, NT * 24], F32)
    hres = sb("hres", [128, NT * 16], F32)
    hpre = sb("hpre", [128, NT * 4], F32)
    hpost = sb("hpost", [128, NT * 4], F32)
    c_lneps = sb("c_lneps", [128, 1], F32)
    c_rmseps = sb("c_rmseps", [128, 1], F32)
    import contextlib as _ctl
    _ph01 = _ctl.ExitStack()
    cosT = _ph01.enter_context(nc.sbuf_tensor("cosT", [128, 1024], F32))
    sinT = _ph01.enter_context(nc.sbuf_tensor("sinT", [128, 1024], F32))
    cosTq = _ph01.enter_context(nc.sbuf_tensor("cosTq", [128, 1024], F32))
    sinTq = _ph01.enter_context(nc.sbuf_tensor("sinTq", [128, 1024], F32))
    xn_fm = _ph01.enter_context(nc.sbuf_tensor("xn_fm", [128, 32 * 1024], BF16))

    sd = Sem(nc, "sd")
    s_pe = Sem(nc, "s_pe")
    s_act = Sem(nc, "s_act")
    s_dve = Sem(nc, "s_dve")
    s_wq = Sem(nc, "s_wq")
    s_x = Sem(nc, "s_x")
    s_cc = Sem(nc, "s_cc")
    s_ao = Sem(nc, "s_ao")
    s_q = Sem(nc, "s_q")
    s_o = Sem(nc, "s_o")
    s_vv = Sem(nc, "s_vv")

    def vs(instr):
        s_vv.inc(instr)
        nc.vector.wait_ge(s_vv.h, s_vv.n)
        return instr

    s_sa = Sem(nc, "s_sa")

    def sa(instr):
        s_sa.inc(instr)
        nc.scalar.wait_ge(s_sa.h, s_sa.n)
        return instr

    def wait_all_engines(sem, val):
        for e in (nc.sync, nc.gpsimd, nc.vector, nc.scalar, nc.tensor):
            e.wait_ge(sem.h, val)

    def tap(name, ap):
        th = nc.dram_tensor("tap_" + name, list(ap.shape), ap.dtype, kind="ExternalOutput").ap()
        sd.inc(nc.sync.dma_start(out=th, in_=ap), 16)

    with nc.sbuf_tensor("p0_i32", [128, 1024], I32) as t_i32, \
         nc.sbuf_tensor("p0_f", [128, 1024], F32) as t_f, \
         nc.sbuf_tensor("p0_ang", [64, 1024], F32) as t_ang, \
         nc.sbuf_tensor("p0_a2", [64, 1024], F32) as t_a2, \
         nc.sbuf_tensor("p0_k", [64, 1024], F32) as t_k, \
         nc.sbuf_tensor("p0_ki", [64, 1024], I32) as t_ki, \
         nc.sbuf_tensor("p0_p0", [1, 1], F32) as t_p0, \
         nc.sbuf_tensor("p0_invf", [1, 64], F32) as t_invf, \
         nc.sbuf_tensor("p0_hb", [1, 24], F32) as t_hb, \
         nc.sbuf_tensor("p0_g", [1, 1], F32) as t_g, \
         nc.sbuf_tensor("p0_irow", [1, 128], F32) as t_irow, \
         nc.psum_tensor("p0_ps", [128, 1024], F32) as p0_ps:

        nc.gpsimd.memset(ident[:], 0.0)
        nc.gpsimd.affine_select(out=ident[:], in_=ident[:], compare_op=ALU.not_equal,
                                fill=1.0, base=0, pattern=[[-1, 128]], channel_multiplier=1)
        nc.gpsimd.memset(ones_col[:], 1.0)
        nc.gpsimd.memset(c_lneps[:], LN_EPS)
        nc.gpsimd.memset(c_rmseps[:], RMS_EPS)
        nc.gpsimd.memset(ones_row[:], 1.0)
        nc.gpsimd.memset(masks[:], 1.0)
        for m in range(4):
            nc.gpsimd.affine_select(out=masks[:, m * 512:(m + 1) * 512],
                                    in_=masks[:, m * 512:(m + 1) * 512],
                                    compare_op=ALU.is_ge, fill=0.0,
                                    base=-128 * m, pattern=[[1, 512]], channel_multiplier=-1)
        nc.gpsimd.iota(t_i32[:], pattern=[[1, 1024]], base=0, channel_multiplier=0)
        nc.gpsimd.tensor_copy(t_f[:], t_i32[:])
        nc.gpsimd.memset(t_irow[:], 0.0)
        for g in range(NT):
            for i in range(4):
                nc.gpsimd.memset(t_irow[0:1, 16 * g + 5 * i: 16 * g + 5 * i + 1], 1.0)
        sd.inc(nc.gpsimd.dma_start(out=t_hb[:], in_=hbase), 16)
        sd.inc(nc.gpsimd.dma_start(out=t_g[:], in_=gate), 16)
        sd.inc(nc.gpsimd.dma_start(out=t_p0[:], in_=pos0), 16)
        sd.inc(nc.gpsimd.dma_start(out=wcatT[:].rearrange("p (c o) -> p c o", c=32),
                                   in_=wcat_t.rearrange("(c p) o -> p c o", p=128)), 16)
        nc.all_engine_barrier()
        wait_all_engines(sd, sd.n)

        nc.gpsimd.iota(t_ki[:, 0:1], pattern=[[0, 1]], base=0, channel_multiplier=1)
        nc.all_engine_barrier()
        nc.vector.tensor_copy(t_k[:, 0:1], t_ki[0:64, 0:1])
        nc.all_engine_barrier()
        nc.scalar.activation(t_k[:, 1:2], t_k[:, 0:1], AF.Exp, scale=-LN2P)
        nc.all_engine_barrier()
        nc.tensor.matmul(p0_ps[0:64, 0:1], ones_row[0:1, 0:64], t_p0[:], start=True, stop=True)
        nc.all_engine_barrier()
        vs(nc.vector.tensor_copy(t_k[:, 2:3], p0_ps[0:64, 0:1]))
        nc.vector.tensor_scalar(t_ang[:], t_f[0:64, :], t_k[:, 2:3], None, ALU.add)
        nc.vector.tensor_scalar(t_ang[:], t_ang[:], t_k[:, 1:2], None, ALU.mult)
        nc.all_engine_barrier()

        for dst, shift in ((sinT, 0.0), (cosT, HALFPI)):
            nc.vector.tensor_scalar(t_a2[:], t_ang[:], shift, None, ALU.add)
            nc.vector.tensor_scalar(t_k[:], t_a2[:], INV2PI, None, ALU.mult)
            nc.vector.tensor_copy(t_ki[0:64, :], t_k[:])
            nc.vector.tensor_copy(t_k[:], t_ki[0:64, :])
            nc.vector.scalar_tensor_tensor(t_a2[:], t_k[:], -CW1, t_a2[:], ALU.mult, ALU.add)
            nc.vector.scalar_tensor_tensor(t_a2[:], t_k[:], -CW2, t_a2[:], ALU.mult, ALU.add)
            nc.all_engine_barrier()
            nc.scalar.activation(dst[0:64, :], t_a2[:], AF.Sin)
            nc.all_engine_barrier()
            nc.vector.tensor_copy(dst[64:128, :], dst[0:64, :])
        nc.vector.tensor_scalar(cosTq[:], cosT[:], QK_SCALE, None, ALU.mult)
        nc.vector.tensor_scalar(sinTq[:], sinT[:], QK_SCALE, None, ALU.mult)

        nc.scalar.activation(t_g[:], t_g[:], AF.Exp, scale=-1.0)
        nc.all_engine_barrier()
        vs(nc.vector.tensor_scalar(t_g[:], t_g[:], 1.0, None, ALU.add))
        nc.vector.reciprocal(t_g[:], t_g[:])
        nc.all_engine_barrier()
        nc.tensor.matmul(p0_ps[:, 0:1], ones_row[:], t_g[:], start=True, stop=True)
        nc.tensor.matmul(p0_ps[:, 4:28], ones_row[:], t_hb[:], start=True, stop=True)
        nc.tensor.matmul(p0_ps[:, 128:256], ones_row[:], t_irow[:], start=True, stop=True)
        nc.all_engine_barrier()
        vs(nc.vector.tensor_copy(g_bc[:], p0_ps[:, 0:1]))
        nc.vector.tensor_copy(hbase_bc[:], p0_ps[:, 4:28])
        nc.vector.tensor_copy(i_rep[:], p0_ps[:, 128:256])
        nc.vector.tensor_scalar(g1m_bc[:], g_bc[:], -1.0, 1.0, ALU.mult, ALU.add)
        nc.all_engine_barrier()
        if _DEBUG:
            for nm, ap in (("cosT", cosT[:]), ("sinT", sinT[:]), ("g_bc", g_bc[:]),
                           ("i_rep", i_rep[:]), ("hbase_bc", hbase_bc[:]), ("masks", masks[:])):
                tap(nm, ap)
            wait_all_engines(sd, sd.n)

    if _BUILD_PHASES >= 1:
        with nc.sbuf_tensor("xt", [128, 2 * 4096], F32) as xt, \
             nc.sbuf_tensor("xnt", [128, 4096], F32) as xnt, \
             nc.sbuf_tensor("xnb", [128, 4096], BF16) as xnb, \
             nc.sbuf_tensor("sq_scr", [128, 4096], F32) as sq_scr, \
             nc.sbuf_tensor("ln_s", [128, 8], F32) as ln_s, \
             nc.sbuf_tensor("wq_sb", [128, 2 * 4096], BF16) as wq_sb, \
             nc.sbuf_tensor("rope_t", [64, 4 * 1024], F32) as rope_t, \
             nc.sbuf_tensor("kvq_out", [128, 2 * 1024], BF16) as kvq_out, \
             nc.psum_tensor("p_tx0", [128, 1024], BF16) as p_tx0, \
             nc.psum_tensor("p_tx1", [128, 1024], BF16) as p_tx1, \
             nc.psum_tensor("p_mm", [128, 2 * 1024], F32) as p_mm, \
             nc.psum_tensor("p_lg0", [128, 24], F32) as p_lg0, \
             nc.psum_tensor("p_lg1", [128, 24], F32) as p_lg1:

            p_tx = (p_tx0, p_tx1)
            p_lg = (p_lg0, p_lg1)

            xt_free = [0, 0]
            tx_evac = [0, 0]
            for tt in range(NT):
                par = tt % 2
                nc.sync.wait_ge(s_dve.h, xt_free[par])
                xtarget = s_x.n + 16
                s_x.inc(nc.sync.dma_start(out=xt[:, par * 4096:(par + 1) * 4096],
                                          in_=x_in[tt * 128:(tt + 1) * 128, :]), 16)
                xtile = xt[:, par * 4096:(par + 1) * 4096]
                nc.vector.wait_ge(s_x.h, xtarget)
                nc.scalar.wait_ge(s_x.h, xtarget)
                vs(nc.vector.reduce_sum(ln_s[:, 0:1], xtile, axis=AX.X))
                vs(nc.vector.tensor_scalar(ln_s[:, 0:1], ln_s[:, 0:1], 1.0 / D, None, ALU.mult))
                sa(nc.scalar.activation(sq_scr[:], xtile, AF.Square, accum_out=ln_s[:, 1:2]))
                s_act.inc(nc.scalar.mul(ln_s[:, 2:3], ln_s[:, 1:2], 1.0 / D))
                nc.vector.wait_ge(s_act.h, s_act.n)
                vs(nc.vector.tensor_tensor(ln_s[:, 3:4], ln_s[:, 0:1], ln_s[:, 0:1], ALU.mult))
                vs(nc.vector.tensor_tensor(ln_s[:, 3:4], ln_s[:, 2:3], ln_s[:, 3:4], ALU.subtract))
                nc.scalar.wait_ge(s_vv.h, s_vv.n)
                sa(nc.scalar.activation(ln_s[:, 4:5], ln_s[:, 3:4], AF.Ln, bias=c_lneps[:]))
                s_act.inc(nc.scalar.activation(stat_r[:, tt:tt + 1], ln_s[:, 4:5], AF.Exp, scale=-0.5))
                nc.vector.wait_ge(s_act.h, s_act.n)
                vs(nc.vector.tensor_tensor(stat_mur[:, tt:tt + 1], ln_s[:, 0:1], stat_r[:, tt:tt + 1], ALU.mult))
                s_dve.inc(nc.vector.tensor_scalar(xnt[:], xtile, stat_r[:, tt:tt + 1],
                                                  stat_mur[:, tt:tt + 1], ALU.mult, ALU.subtract))
                xt_free[par] = s_dve.n
                nc.scalar.wait_ge(s_dve.h, s_dve.n)
                sa(nc.scalar.activation(sq_scr[:], xnt[:], AF.Square, accum_out=ln_s[:, 5:6]))
                sa(nc.scalar.activation(ln_s[:, 6:7], ln_s[:, 5:6], AF.Ln, scale=1.0 / D, bias=c_rmseps[:]))
                s_act.inc(nc.scalar.activation(stat_rmsr[:, tt:tt + 1], ln_s[:, 6:7], AF.Exp, scale=-0.5))
                nc.vector.wait_ge(s_act.h, s_act.n)
                s_dve.inc(nc.vector.tensor_copy(xnb[:], xnt[:]))
                nc.tensor.wait_ge(s_dve.h, s_dve.n)
                xfm_ap = xn_fm[:]
                for b in range(4):
                    tp = b % 2
                    nc.tensor.wait_ge(s_dve.h, tx_evac[tp])
                    for i in range(8):
                        kc = b * 8 + i
                        s_pe.inc(nc.tensor.transpose(p_tx[tp][:, i * 128:(i + 1) * 128],
                                                     xnb[:, kc * 128:(kc + 1) * 128], ident[:]),
                                 1 if i == 7 else 0)
                    nc.vector.wait_ge(s_pe.h, s_pe.n)
                    dst = APc(xfm_ap.tensor, xfm_ap.offset + b * 8 * 1024 + tt * 128,
                              [[32 * 1024, 128], [1024, 8], [1, 128]])
                    s_dve.inc(nc.vector.tensor_copy(
                        dst, p_tx[tp][:].rearrange("p (i j) -> p i j", j=128)))
                    tx_evac[tp] = s_dve.n
            nc.all_engine_barrier()
            if _DEBUG:
                for nm, ap in (("stat_r", stat_r[:]), ("stat_mur", stat_mur[:]),
                               ("stat_rmsr", stat_rmsr[:]), ("xn_fm", xn_fm[:])):
                    tap(nm, ap)
                wait_all_engines(sd, sd.n)

            mm_evac = [0, 0]
            out_free = [0, 0]
            wq_used = [0, 0]
            m_order = list(range(16, 48)) + list(range(16))
            for m_idx, m in enumerate(m_order):
                par = m_idx % 2
                nc.gpsimd.wait_ge(s_pe.h, wq_used[par])
                wtarget = s_wq.n + 16
                s_wq.inc(nc.gpsimd.dma_start(out=wq_sb[:, par * 4096:(par + 1) * 4096],
                                             in_=wqt[m * 128:(m + 1) * 128, :]), 16)
                nc.tensor.wait_ge(s_wq.h, wtarget)
                nc.tensor.wait_ge(s_dve.h, mm_evac[par])
                for nh in range(2):
                    for kc in range(32):
                        last = (nh == 1 and kc == 31)
                        s_pe.inc(nc.tensor.matmul(
                            p_mm[:, par * 1024 + nh * 512: par * 1024 + (nh + 1) * 512],
                            wq_sb[:, par * 4096 + kc * 128: par * 4096 + (kc + 1) * 128],
                            xn_fm[:, kc * 1024 + nh * 512: kc * 1024 + (nh + 1) * 512],
                            start=(kc == 0), stop=(kc == 31)), 1 if last else 0)
                wq_used[par] = s_pe.n
                pm = p_mm[:, par * 1024:(par + 1) * 1024]
                ob = kvq_out[:, par * 1024:(par + 1) * 1024]
                nc.vector.wait_ge(s_pe.h, s_pe.n)
                nc.vector.wait_ge(sd.h, out_free[par])
                if m < 8 or m >= 16:
                    isq = m >= 16
                    ct, st = (cosTq, sinTq) if isq else (cosT, sinT)
                    r = rope_t
                    nc.vector.tensor_tensor(r[:, 0:1024], pm[0:64, :], ct[0:64, :], ALU.mult)
                    nc.vector.tensor_tensor(r[:, 1024:2048], pm[64:128, :], st[0:64, :], ALU.mult)
                    nc.vector.tensor_tensor(r[:, 2048:3072], pm[0:64, :], st[0:64, :], ALU.mult)
                    vs(nc.vector.tensor_tensor(r[:, 3072:4096], pm[64:128, :], ct[0:64, :], ALU.mult))
                    nc.vector.tensor_tensor(ob[0:64, :], r[:, 0:1024], r[:, 1024:2048], ALU.subtract)
                    s_dve.inc(nc.vector.tensor_tensor(ob[64:128, :], r[:, 2048:3072], r[:, 3072:4096], ALU.add))
                    if isq:
                        h = m - 16
                        dst = a2a_all_in[768 * (h // 4) + 256 + 128 * (h % 4):
                                         768 * (h // 4) + 256 + 128 * (h % 4) + 128, :]
                    else:
                        dst = a2a_all_in[768 * m: 768 * m + 128, :]
                else:
                    g = m - 8
                    s_dve.inc(nc.vector.tensor_copy(ob, pm))
                    dst = a2a_all_in[768 * g + 128: 768 * g + 256, :]
                mm_evac[par] = s_dve.n
                nc.sync.wait_ge(s_dve.h, s_dve.n)
                sd.inc(nc.sync.dma_start(out=dst, in_=ob), 16)
                if _DEBUG and m in (0, 8):
                    nc.all_engine_barrier()
                    tap("wq_slab_%d" % m, wq_sb[:, par * 4096:(par + 1) * 4096])
                    tap("ob_%d" % m, ob)
                    wait_all_engines(sd, sd.n)
                    nc.all_engine_barrier()
                out_free[par] = sd.n
            nc.gpsimd.wait_ge(sd.h, sd.n)
            with allow_ici_all_to_all():
                s_cc.inc(nc.gpsimd.collective_compute(
                    "AllToAll", ALU.bypass, replica_groups=[list(range(NCORES))],
                    ins=[a2a_all_in], outs=[a2a_all_out]))

            lg_evac = [0, 0]
            for tt in range(NT):
                par = tt % 2
                nc.tensor.wait_ge(s_vv.h, lg_evac[par])
                for kc in range(32):
                    s_pe.inc(nc.tensor.matmul(
                        p_lg[par][:],
                        xn_fm[:, kc * 1024 + tt * 128: kc * 1024 + (tt + 1) * 128],
                        wcatT[:, kc * 24:(kc + 1) * 24],
                        start=(kc == 0), stop=(kc == 31)), 1 if kc == 31 else 0)
                nc.vector.wait_ge(s_pe.h, s_pe.n)
                vs(nc.vector.tensor_scalar(logits[:, tt * 24:(tt + 1) * 24],
                                            p_lg[par][:],
                                            stat_rmsr[:, tt:tt + 1], None, ALU.mult))
                vs(nc.vector.tensor_tensor(logits[:, tt * 24:(tt + 1) * 24],
                                            logits[:, tt * 24:(tt + 1) * 24],
                                            hbase_bc[:], ALU.add))
                lg_evac[par] = s_vv.n
                out_free[par] = sd.n
            nc.gpsimd.wait_ge(sd.h, sd.n)
            with allow_ici_all_to_all():
                s_cc.inc(nc.gpsimd.collective_compute(
                    "AllToAll", ALU.bypass, replica_groups=[list(range(NCORES))],
                    ins=[a2a_all_in], outs=[a2a_all_out]))
            if _DEBUG:
                wait_all_engines(s_cc, s_cc.n)
                tap("a2a_q_in", a2a_q_in)
                tap("a2a_kv_in", a2a_kv_in)
                tap("a2a_q_out", a2a_q_out)
                tap("a2a_kv_out", a2a_kv_out)
                wait_all_engines(sd, sd.n)
            nc.all_engine_barrier()
            if _DEBUG:
                tap("logits", logits[:])
                wait_all_engines(sd, sd.n)

    _ph01.close()

    if _BUILD_PHASES >= 2:
        with nc.sbuf_tensor("sk_rs2", [128, 64], F32) as rs2, \
             nc.sbuf_tensor("sk_rs", [128, 32], F32) as rs:
            Z = logits[:].rearrange("p (g o) -> p g o", o=24)
            nc.scalar.activation(hres[:], Z[:, :, 0:16], AF.Exp, scale=10.0)
            nc.all_engine_barrier()
            H = hres[:].rearrange("p (g i j) -> p g i j", i=4, j=4)
            Hc = hres[:].rearrange("p (g i j) -> p g j i", i=4, j=4)
            R2i = rs2[:].rearrange("p (g i h) -> p g i h", i=4, h=2)
            Ri = rs[:].rearrange("p (g i) -> p g i", i=4)
            Rib = bcast_j(Ri, 4)
            R2j = rs2[:].rearrange("p (g j h) -> p g j h", j=4, h=2)
            Rj = rs[:].rearrange("p (g j) -> p g j", j=4)
            Rjb = bcast_j(Rj, 4)
            for it in range(50):
                vs(nc.vector.tensor_tensor(R2i, H[:, :, :, 0::2], H[:, :, :, 1::2], ALU.add))
                vs(nc.vector.tensor_tensor(Ri, R2i[:, :, :, 0], R2i[:, :, :, 1], ALU.add))
                vs(nc.vector.reciprocal(rs[:], rs[:]))
                vs(nc.vector.tensor_tensor(H, H, Rib, ALU.mult))
                vs(nc.vector.tensor_tensor(R2j, Hc[:, :, :, 0::2], Hc[:, :, :, 1::2], ALU.add))
                vs(nc.vector.tensor_tensor(Rj, R2j[:, :, :, 0], R2j[:, :, :, 1], ALU.add))
                vs(nc.vector.reciprocal(rs[:], rs[:]))
                vs(nc.vector.tensor_tensor(Hc, Hc, Rjb, ALU.mult))
            vs(nc.vector.tensor_scalar(hres[:], hres[:], g_bc[:], None, ALU.mult))
            vs(nc.vector.scalar_tensor_tensor(hres[:], i_rep[:], g1m_bc[:], hres[:], ALU.mult, ALU.add))
            for dst, off in ((hpre, 16), (hpost, 20)):
                nc.all_engine_barrier()
                nc.scalar.activation(dst[:], Z[:, :, off:off + 4], AF.Exp)
                nc.all_engine_barrier()
                Dv = dst[:].rearrange("p (g j) -> p g j", j=4)
                S2 = rs2[:, 0:16].rearrange("p (g h) -> p g h", h=2)
                S1 = rs[:, 0:8].rearrange("p (g o) -> p g o", o=1)
                vs(nc.vector.tensor_tensor(S2, Dv[:, :, 0::2], Dv[:, :, 1::2], ALU.add))
                vs(nc.vector.tensor_tensor(S1, S2[:, :, 0:1], S2[:, :, 1:2], ALU.add))
                vs(nc.vector.reciprocal(rs[:, 0:8], rs[:, 0:8]))
                nc.vector.tensor_tensor(Dv, Dv, bcast_j(S1[:, :, 0], 4), ALU.mult)
            nc.all_engine_barrier()
            if _DEBUG:
                for nm, ap in (("hres_f", hres[:]), ("hpre", hpre[:]), ("hpost", hpost[:])):
                    tap(nm, ap)
                wait_all_engines(sd, sd.n)

    _ph34 = _ctl.ExitStack()
    wbr_sb = _ph34.enter_context(nc.sbuf_tensor("wbr_sb", [128, 8 * 1024], BF16))
    wtgt4a = [0]
    if _BUILD_PHASES >= 3:
        with nc.sbuf_tensor("K_sb", [128, 8 * 1024], BF16) as K_sb, \
             nc.sbuf_tensor("Vf_sb", [128, 8 * 1024], BF16) as Vf_sb, \
             nc.sbuf_tensor("Vt_sb", [128, 8 * 1024], BF16) as Vt_sb, \
             nc.sbuf_tensor("q_sb", [128, 2 * 8192], BF16) as q_sb, \
             nc.sbuf_tensor("e_sb", [128, 2 * 1024], BF16) as e_sb, \
             nc.sbuf_tensor("den_r", [1, 512], F32) as den_r, \
             nc.sbuf_tensor("bc_sb", [128, 512], F32) as bc_sb, \
             nc.sbuf_tensor("ao_t", [128, 2 * 512], BF16) as ao_t:

            wait_all_engines(s_cc, 1)
            kvv = a2a_all_out.rearrange("(r s) t -> s r t", s=768)
            sd.inc(nc.sync.dma_start(out=K_sb[:].rearrange("d (r t) -> d r t", r=8),
                                     in_=kvv[0:128, :, :]), 16)
            sd.inc(nc.sync.dma_start(out=Vf_sb[:].rearrange("d (r t) -> d r t", r=8),
                                     in_=kvv[128:256, :, :]), 16)
            wtgt4a[0] = s_wq.n + 8 * 16
            for ch in range(8):
                s_wq.inc(nc.gpsimd.dma_start(out=wbr_sb[:, ch * 1024:(ch + 1) * 1024],
                                             in_=wbr[ch * 128:(ch + 1) * 128, :]), 16)
            qtgt = [0, 0]
            qtgt[0] = s_q.n + 16
            s_q.inc(nc.sync.dma_start(out=q_sb[:, 0:8192].rearrange("d (r t) -> d r t", r=8),
                                      in_=kvv[256:384, :, :]), 16)
            wait_all_engines(sd, sd.n)
            with nc.psum_tensor("p_vt0", [128, 1024], BF16) as p_vt0, \
                 nc.psum_tensor("p_vt1", [128, 1024], BF16) as p_vt1:
                p_vt = (p_vt0, p_vt1)
                vt_evac = [0, 0]
                for b in range(8):
                    tp = b % 2
                    nc.tensor.wait_ge(s_dve.h, vt_evac[tp])
                    for i in range(8):
                        ch = b * 8 + i
                        s_pe.inc(nc.tensor.transpose(p_vt[tp][:, i * 128:(i + 1) * 128],
                                                     Vf_sb[:, ch * 128:(ch + 1) * 128], ident[:]),
                                 1 if i == 7 else 0)
                    nc.vector.wait_ge(s_pe.h, s_pe.n)
                    s_dve.inc(nc.vector.tensor_copy(Vt_sb[:, b * 1024:(b + 1) * 1024], p_vt[tp][:]))
                    vt_evac[tp] = s_dve.n
                nc.all_engine_barrier()

            with nc.psum_tensor("ps_s0", [128, 1024], F32) as ps_s0, \
                 nc.psum_tensor("ps_s1", [128, 1024], F32) as ps_s1, \
                 nc.psum_tensor("ps_pv0", [128, 512], F32) as ps_pv0, \
                 nc.psum_tensor("ps_pv1", [128, 512], F32) as ps_pv1, \
                 nc.psum_tensor("ps_den", [1, 512], F32) as ps_den, \
                 nc.psum_tensor("ps_bc", [128, 512], F32) as ps_bc:
                ps_s = (ps_s0, ps_s1)
                ps_pv = (ps_pv0, ps_pv1)
                exp_done = [0, 0]
                e_used = [0, 0]
                mask_done = [0, 0]
                pv_free = [0, 0]
                den_free = 0
                bc_free = 0
                ao_free = [0, 0]
                q_scores_done = [0, 0]
                unit_idx = 0
                for hh in range(4):
                    qpar = hh % 2
                    if hh < 3:
                        nc.sync.wait_ge(s_pe.h, q_scores_done[1 - qpar])
                        qtgt[1 - qpar] = s_q.n + 16
                        s_q.inc(nc.sync.dma_start(
                            out=q_sb[:, (1 - qpar) * 8192:(2 - qpar) * 8192].rearrange("d (r t) -> d r t", r=8),
                            in_=kvv[256 + 128 * (hh + 1):256 + 128 * (hh + 2), :, :]), 16)
                    nc.tensor.wait_ge(s_q.h, qtgt[qpar])
                    for qb in range(16):
                        kcs = 4 * (qb + 1)
                        pvb = unit_idx % 2
                        npairs = kcs // 2
                        scores_cnt = [0, 0]
                        mm_done = 0

                        def emit_scores(up):
                            sp = up % 2
                            nc.tensor.wait_ge(s_act.h, exp_done[sp])
                            u = 2 * up
                            nc.tensor.matmul(ps_s[sp][:, 0:512],
                                             K_sb[:, u * 128:(u + 1) * 128],
                                             q_sb[:, qpar * 8192 + qb * 512: qpar * 8192 + (qb + 1) * 512],
                                             start=True, stop=True)
                            s_pe.inc(nc.tensor.matmul(ps_s[sp][:, 512:1024],
                                             K_sb[:, (u + 1) * 128:(u + 2) * 128],
                                             q_sb[:, qpar * 8192 + qb * 512: qpar * 8192 + (qb + 1) * 512],
                                             start=True, stop=True))
                            scores_cnt[sp] = s_pe.n
                            nc.scalar.wait_ge(s_pe.h, scores_cnt[sp])
                            nc.scalar.wait_ge(s_pe.h, e_used[sp])
                            s_act.inc(nc.scalar.activation(e_sb[:, sp * 1024:(sp + 1) * 1024],
                                                           ps_s[sp][:], AF.Exp))
                            exp_done[sp] = s_act.n
                            diag0 = u - 4 * qb
                            diag1 = u + 1 - 4 * qb
                            m0 = 0 <= diag0 <= 3
                            m1 = 0 <= diag1 <= 3
                            if m0 or m1:
                                nc.vector.wait_ge(s_act.h, exp_done[sp])
                            if m0:
                                i0 = nc.vector.tensor_tensor(
                                    e_sb[:, sp * 1024: sp * 1024 + 512],
                                    e_sb[:, sp * 1024: sp * 1024 + 512],
                                    masks[:, diag0 * 512:(diag0 + 1) * 512], ALU.mult)
                                if not m1:
                                    s_dve.inc(i0)
                                    mask_done[sp] = s_dve.n
                            if m1:
                                s_dve.inc(nc.vector.tensor_tensor(
                                    e_sb[:, sp * 1024 + 512: sp * 1024 + 1024],
                                    e_sb[:, sp * 1024 + 512: sp * 1024 + 1024],
                                    masks[:, diag1 * 512:(diag1 + 1) * 512], ALU.mult))
                                mask_done[sp] = s_dve.n
                            return m0 or m1

                        def emit_denpv(up, masked):
                            nonlocal mm_done
                            sp = up % 2
                            u = 2 * up
                            first = (up == 0)
                            last = (up == npairs - 1)
                            if masked:
                                nc.tensor.wait_ge(s_dve.h, mask_done[sp])
                            else:
                                nc.tensor.wait_ge(s_act.h, exp_done[sp])
                            if first:
                                nc.tensor.wait_ge(s_dve.h, den_free)
                                nc.tensor.wait_ge(s_dve.h, pv_free[pvb])
                            nc.tensor.matmul(ps_den[:], ones_col[:],
                                             e_sb[:, sp * 1024: sp * 1024 + 512],
                                             start=first, stop=False)
                            nc.tensor.matmul(ps_den[:], ones_col[:],
                                             e_sb[:, sp * 1024 + 512: sp * 1024 + 1024],
                                             start=False, stop=last)
                            nc.tensor.matmul(ps_pv[pvb][:], Vt_sb[:, u * 128:(u + 1) * 128],
                                             e_sb[:, sp * 1024: sp * 1024 + 512],
                                             start=first, stop=False)
                            s_pe.inc(nc.tensor.matmul(ps_pv[pvb][:], Vt_sb[:, (u + 1) * 128:(u + 2) * 128],
                                             e_sb[:, sp * 1024 + 512: sp * 1024 + 1024],
                                             start=False, stop=last))
                            e_used[sp] = s_pe.n
                            if last:
                                mm_done = s_pe.n

                        mflags = [False] * npairs
                        mflags[0] = emit_scores(0)
                        for up in range(1, npairs):
                            mflags[up] = emit_scores(up)
                            emit_denpv(up - 1, mflags[up - 1])
                        emit_denpv(npairs - 1, mflags[npairs - 1])
                        nc.vector.wait_ge(s_pe.h, mm_done)
                        s_dve.inc(nc.vector.reciprocal(den_r[:], ps_den[:]))
                        den_free = s_dve.n
                        nc.vector.wait_ge(s_dve.h, den_free)
                        nc.tensor.wait_ge(s_dve.h, den_free)
                        nc.tensor.wait_ge(s_dve.h, bc_free)
                        s_pe.inc(nc.tensor.matmul(ps_bc[:], ones_row[:], den_r[:], start=True, stop=True))
                        nc.vector.wait_ge(s_pe.h, s_pe.n)
                        vs(nc.vector.tensor_copy(bc_sb[:], ps_bc[:]))
                        nc.vector.wait_ge(s_ao.h, ao_free[pvb])
                        s_dve.inc(nc.vector.tensor_tensor(ao_t[:, pvb * 512:(pvb + 1) * 512],
                                                          ps_pv[pvb][:], bc_sb[:], ALU.mult))
                        pv_free[pvb] = s_dve.n
                        bc_free = s_dve.n
                        nc.sync.wait_ge(s_dve.h, s_dve.n)
                        s_ao.inc(nc.sync.dma_start(
                            out=a2a_ao_in[512 * (qb // 2) + 128 * hh: 512 * (qb // 2) + 128 * (hh + 1),
                                          (qb % 2) * 512:(qb % 2 + 1) * 512],
                            in_=ao_t[:, pvb * 512:(pvb + 1) * 512]), 16)
                        ao_free[pvb] = s_ao.n
                        if _DEBUG and hh == 0 and qb <= 1:
                            nc.all_engine_barrier()
                            wait_all_engines(s_ao, s_ao.n)
                            tap("e_sb_%d" % qb, e_sb[:])
                            tap("den_%d" % qb, den_r[:])
                            tap("bc_%d" % qb, bc_sb[:])
                            tap("ao_%d" % qb, ao_t[:])
                            if qb == 0:
                                tap("K_sb", K_sb[:])
                                tap("Vt_sb", Vt_sb[:])
                                tap("q0", q_sb[:, 0:8192])
                            wait_all_engines(sd, sd.n)
                            nc.all_engine_barrier()
                        unit_idx += 1
                    q_scores_done[qpar] = s_pe.n
                nc.gpsimd.wait_ge(s_ao.h, s_ao.n)
                with allow_ici_all_to_all():
                    s_cc.inc(nc.gpsimd.collective_compute(
                        "AllToAll", ALU.bypass, replica_groups=[list(range(NCORES))],
                        ins=[a2a_ao_in], outs=[a2a_ao_out]))
        nc.all_engine_barrier()

    if _BUILD_PHASES >= 4:
        with nc.sbuf_tensor("x4", [128, 2 * 4096], F32) as x4, \
             nc.sbuf_tensor("xn4", [128, 4096], F32) as xn4, \
             nc.sbuf_tensor("xp4", [128, 1024], F32) as xp4, \
             nc.sbuf_tensor("xpb4", [128, 1024], BF16) as xpb4, \
             nc.sbuf_tensor("xpf4", [128, 1024], BF16) as xpf4, \
             nc.sbuf_tensor("y4", [128, 1024], F32) as y4, \
             nc.sbuf_tensor("go4", [128, 2 * 4096], F32) as go4, \
             nc.psum_tensor("p_y", [128, 1024], F32) as p_y, \
             nc.psum_tensor("p_xt0", [128, 1024], BF16) as p_xt0, \
             nc.psum_tensor("p_xt1", [128, 1024], BF16) as p_xt1:
            p_xt = (p_xt0, p_xt1)
            wtgt = wtgt4a[0]
            x4_free = [0, 0]
            go_free = [0, 0]
            xt_evac = [0, 0]
            for tt in range(NT):
                par = tt % 2
                nc.sync.wait_ge(s_dve.h, x4_free[par])
                xtgt = s_x.n + 16
                s_x.inc(nc.sync.dma_start(out=x4[:, par * 4096:(par + 1) * 4096],
                                          in_=x_in[tt * 128:(tt + 1) * 128, :]), 16)
                xtile = x4[:, par * 4096:(par + 1) * 4096]
                nc.vector.wait_ge(s_x.h, xtgt)
                s_dve.inc(nc.vector.tensor_scalar(xn4[:], xtile, stat_r[:, tt:tt + 1],
                                                  stat_mur[:, tt:tt + 1], ALU.mult, ALU.subtract))
                x4_free[par] = s_dve.n
                nc.vector.tensor_scalar(xp4[:], xn4[:, 0:1024], hpre[:, 4 * tt: 4 * tt + 1], None, ALU.mult)
                for n in range(1, 4):
                    instr = nc.vector.scalar_tensor_tensor(xp4[:], xn4[:, n * 1024:(n + 1) * 1024],
                                                           hpre[:, 4 * tt + n: 4 * tt + n + 1], xp4[:],
                                                           ALU.mult, ALU.add)
                    if n == 3:
                        vs(instr)
                s_dve.inc(nc.vector.tensor_copy(xpb4[:], xp4[:]))
                nc.tensor.wait_ge(s_dve.h, s_dve.n)
                tp = tt % 2
                nc.tensor.wait_ge(s_dve.h, xt_evac[tp])
                for ch in range(8):
                    s_pe.inc(nc.tensor.transpose(p_xt[tp][:, ch * 128:(ch + 1) * 128],
                                                 xpb4[:, ch * 128:(ch + 1) * 128], ident[:]),
                             1 if ch == 7 else 0)
                nc.vector.wait_ge(s_pe.h, s_pe.n)
                s_dve.inc(nc.vector.tensor_copy(xpf4[:], p_xt[tp][:]))
                xt_evac[tp] = s_dve.n
                nc.tensor.wait_ge(s_wq.h, wtgt)
                nc.tensor.wait_ge(s_dve.h, s_dve.n)
                for nh in range(2):
                    for ch in range(8):
                        s_pe.inc(nc.tensor.matmul(p_y[:, nh * 512:(nh + 1) * 512],
                                                  xpf4[:, ch * 128:(ch + 1) * 128],
                                                  wbr_sb[:, ch * 1024 + nh * 512: ch * 1024 + (nh + 1) * 512],
                                                  start=(ch == 0), stop=(ch == 7)),
                                 1 if (nh == 1 and ch == 7) else 0)
                nc.vector.wait_ge(s_pe.h, s_pe.n)
                s_dve.inc(nc.vector.tensor_copy(y4[:], p_y[:]))
                nc.vector.wait_ge(s_ao.h, go_free[par])
                go = go4[:, par * 4096:(par + 1) * 4096]
                for i in range(4):
                    nc.vector.tensor_scalar(go[:, i * 1024:(i + 1) * 1024], xn4[:, 0:1024],
                                            hres[:, 16 * tt + 4 * i: 16 * tt + 4 * i + 1], None, ALU.mult)
                    for j in range(1, 4):
                        nc.vector.scalar_tensor_tensor(go[:, i * 1024:(i + 1) * 1024],
                                                       xn4[:, j * 1024:(j + 1) * 1024],
                                                       hres[:, 16 * tt + 4 * i + j: 16 * tt + 4 * i + j + 1],
                                                       go[:, i * 1024:(i + 1) * 1024], ALU.mult, ALU.add)
                    s_dve.inc(nc.vector.scalar_tensor_tensor(go[:, i * 1024:(i + 1) * 1024],
                                                   y4[:], hpost[:, 4 * tt + i: 4 * tt + i + 1],
                                                   go[:, i * 1024:(i + 1) * 1024], ALU.mult, ALU.add))
                nc.scalar.wait_ge(s_dve.h, s_dve.n)
                s_act.inc(nc.scalar.activation(go, go, AF.Silu))
                nc.sync.wait_ge(s_act.h, s_act.n)
                s_ao.inc(nc.sync.dma_start(out=g_spill[tt * 128:(tt + 1) * 128, :], in_=go), 16)
                go_free[par] = s_ao.n
        g_spill_target = s_ao.n
        nc.all_engine_barrier()
    _ph34.close()

    if _BUILD_PHASES >= 5:
        with nc.sbuf_tensor("aosb", [128, 32 * 1024], BF16) as aosb, \
             nc.sbuf_tensor("wo_sb", [128, 3 * 32 * 512], BF16) as wo_sb, \
             nc.sbuf_tensor("xs4", [128, 4 * 512], F32) as xs4, \
             nc.sbuf_tensor("gs4", [128, 4 * 512], F32) as gs4, \
             nc.sbuf_tensor("ot4", [128, 4 * 512], F32) as ot4, \
             nc.psum_tensor("p_o0", [128, 512], F32) as p_o0, \
             nc.psum_tensor("p_o1", [128, 512], F32) as p_o1, \
             nc.psum_tensor("p_o2", [128, 512], F32) as p_o2, \
             nc.psum_tensor("p_o3", [128, 512], F32) as p_o3:
            p_o = (p_o0, p_o1, p_o2, p_o3)
            wait_all_engines(s_cc, 2)
            nc.sync.wait_ge(s_ao.h, g_spill_target)
            aotgt = sd.n + 16
            sd.inc(nc.sync.dma_start(out=aosb[:].rearrange("p (c t) -> p c t", c=32),
                                     in_=a2a_ao_out.rearrange("(c p) t -> p c t", p=128)), 16)
            nc.tensor.wait_ge(sd.h, aotgt)
            wo_tgt = [0, 0, 0]
            o_evac = [0, 0, 0, 0]
            st_free = [0, 0, 0, 0]
            ot_free = [0, 0, 0, 0]
            wo_used = [0, 0, 0]
            for og in range(8):
                wpar = og % 3
                nc.gpsimd.wait_ge(s_pe.h, wo_used[wpar])
                wo_tgt[wpar] = s_wq.n + 16
                s_wq.inc(nc.gpsimd.dma_start(
                    out=wo_sb[:, wpar * 16384:(wpar + 1) * 16384],
                    in_=wot[og * 128:(og + 1) * 128, :]), 16)
                nc.tensor.wait_ge(s_wq.h, wo_tgt[wpar])
                for qt in range(NT):
                    opar = qt % 4
                    nc.tensor.wait_ge(s_dve.h, o_evac[opar])
                    for dc in range(32):
                        s_pe.inc(nc.tensor.matmul(
                            p_o[opar][:],
                            aosb[:, dc * 1024 + qt * 128: dc * 1024 + (qt + 1) * 128],
                            wo_sb[:, wpar * 16384 + dc * 512: wpar * 16384 + (dc + 1) * 512],
                            start=(dc == 0), stop=(dc == 31)), 1 if dc == 31 else 0)
                    mmcnt = s_pe.n
                    nc.sync.wait_ge(s_dve.h, st_free[opar])
                    stgt = s_x.n + 32
                    s_x.inc(nc.sync.dma_start(out=xs4[:, opar * 512:(opar + 1) * 512],
                                              in_=x_in[qt * 128:(qt + 1) * 128, og * 512:(og + 1) * 512]), 16)
                    s_x.inc(nc.sync.dma_start(out=gs4[:, opar * 512:(opar + 1) * 512],
                                              in_=g_spill[qt * 128:(qt + 1) * 128, og * 512:(og + 1) * 512]), 16)
                    nc.vector.wait_ge(s_pe.h, mmcnt)
                    nc.vector.wait_ge(s_x.h, stgt)
                    nc.vector.wait_ge(s_o.h, ot_free[opar])
                    vs(nc.vector.tensor_tensor(ot4[:, opar * 512:(opar + 1) * 512],
                                               p_o[opar][:], gs4[:, opar * 512:(opar + 1) * 512], ALU.mult))
                    s_dve.inc(nc.vector.tensor_tensor(ot4[:, opar * 512:(opar + 1) * 512],
                                            ot4[:, opar * 512:(opar + 1) * 512],
                                            xs4[:, opar * 512:(opar + 1) * 512], ALU.add))
                    o_evac[opar] = s_dve.n
                    st_free[opar] = s_dve.n
                    nc.sync.wait_ge(s_dve.h, s_dve.n)
                    s_o.inc(nc.sync.dma_start(out=out[qt * 128:(qt + 1) * 128, og * 512:(og + 1) * 512],
                                              in_=ot4[:, opar * 512:(opar + 1) * 512]), 16)
                    ot_free[opar] = s_o.n
                wo_used[wpar] = s_pe.n
            wait_all_engines(s_o, s_o.n)
        nc.all_engine_barrier()

    wait_all_engines(sd, sd.n)
    if _BUILD_PHASES < 5:
        if s_cc.n:
            wait_all_engines(s_cc, s_cc.n)
        with nc.sbuf_tensor("zout", [128, D], F32) as zout:
            nc.vector.memset(zout[:], 0.0)
            nc.all_engine_barrier()
            for tt in range(NT):
                sd.inc(nc.sync.dma_start(out=out[tt * 128:(tt + 1) * 128, :], in_=zout[:]), 16)
            wait_all_engines(sd, sd.n)
    return nc, out


def _host_layouts(inputs):
    w_qkv = np.asarray(inputs["L0_attn_w_qkv"], np.float32)
    w_o = np.asarray(inputs["L0_attn_w_o"], np.float32)
    W_res = np.asarray(inputs["L0_ffn_W_res"], np.float32)
    W_pre = np.asarray(inputs["L0_ffn_W_pre"], np.float32)
    W_post = np.asarray(inputs["L0_ffn_W_post"], np.float32)

    deint = np.concatenate([np.arange(0, 128, 2), np.arange(1, 128, 2)])
    cols = []
    for g in range(8):
        cols.append(4096 + 128 * g + deint)
    for g in range(8):
        cols.append(5120 + 128 * g + np.arange(128))
    for h in range(32):
        cols.append(128 * h + deint)
    col_order = np.concatenate(cols)
    wq = w_qkv[:, col_order]
    wqt = np.ascontiguousarray(
        wq.reshape(32, 128, 48, 128).transpose(2, 1, 0, 3).reshape(48 * 128, 32 * 128))
    wot = np.ascontiguousarray(
        w_o.reshape(32, 128, 8, 512).transpose(2, 1, 0, 3).reshape(1024, 16384))
    wcat_t = np.ascontiguousarray(np.concatenate([W_res, W_pre, W_post], 0).T)
    hbase = np.ascontiguousarray(np.concatenate([
        np.asarray(inputs["L0_ffn_H_res_base"], np.float32).reshape(-1),
        np.asarray(inputs["L0_ffn_H_pre_base"], np.float32).reshape(-1),
        np.asarray(inputs["L0_ffn_H_post_base"], np.float32).reshape(-1)])[None, :])
    gate = np.asarray(inputs["L0_ffn_gate"], np.float32).reshape(1, 1)
    wbr = np.ascontiguousarray(np.asarray(inputs["L0_ffn_W_branch"], np.float32))
    return wqt, wot, wbr, wcat_t, hbase, gate


def make_in_maps(inputs):
    x = np.asarray(inputs["L0_attn_x"], np.float32).reshape(SEQ, D)
    wqt, wot, wbr, wcat_t, hbase, gate = _host_layouts(inputs)
    in_maps = []
    for c in range(NCORES):
        in_maps.append({
            "x": np.ascontiguousarray(x[c * TC:(c + 1) * TC]),
            "pos0": np.array([[c * TC]], np.float32),
            "wqt": wqt, "wot": wot, "wbr": wbr, "wcat_t": wcat_t,
            "hbase": hbase, "gate": gate,
        })
    return in_maps


def kernel(**inputs):
    nc, _ = build_program()
    in_maps = make_in_maps(inputs)
    res = run_bass_kernel_spmd(nc, in_maps, core_ids=list(range(NCORES)))
    out = np.empty((SEQ, D), np.float32)
    for c in range(NCORES):
        out[c * TC:(c + 1) * TC] = res.results[c]["out"]
    return out.reshape(1, SEQ, D)
